# version 06
# baseline (speedup 1.0000x reference)
import numpy as np
import ml_dtypes
import concourse.bass as bass
import concourse.mybir as mybir
from concourse.bass_utils import run_bass_kernel_spmd

F32, BF16 = mybir.dt.float32, mybir.dt.bfloat16
AF = mybir.ActivationFunctionType
ALU = mybir.AluOpType
AX = mybir.AxisListType

T, D, H, DH, L, E, FH = 4096, 1024, 16, 64, 256, 8, 1024
NT = T // 128
CHUNK = 1024
NCH = T // CHUNK
CT = CHUNK // 128
KD = D // 128
KL = L // 128
EPS = 1e-5
INV2PI = float(np.float32(1.0 / (2 * np.pi)))
MAGIC = float(np.float32(3 * 2.0 ** 22))
RC1 = float(np.float32(6.28125))
RC2 = float(np.float32(2 * np.pi - 6.28125))
HALFPI = float(np.float32(np.pi / 2))

ENGINES = ("tensor", "vector", "scalar", "gpsimd", "sync")


class PB:
    def __init__(self, nc, stack):
        self.nc, self.stack = nc, stack
        self.ops = {e: [] for e in ENGINES}
        self.semh, self.cnt = {}, {}
        self.free = {}

    def sem(self, name):
        self.semh[name] = self.stack.enter_context(self.nc.semaphore(name))
        self.cnt[name] = 0

    def emit(self, eng, fn=None, wait=(), inc=None, inc_n=1):
        post = None
        if inc is not None:
            self.cnt[inc] += inc_n
            post = self.cnt[inc]
        w = [(self.semh[s], v) for (s, v) in wait if v > 0]
        ih = self.semh[inc] if inc is not None else None

        def run(e, w=w, fn=fn, ih=ih, n=inc_n):
            for sh, v in w:
                e.wait_ge(sh, v)
            if fn is not None:
                ins = fn(e)
                if ih is not None:
                    ins.then_inc(ih, n)
            elif ih is not None:
                e.sem_inc(ih, n)

        self.ops[eng].append(run)
        return post

    def dma(self, eng, out, in_, wait=(), inc=None):
        self.cnt[inc] += 16
        post = self.cnt[inc]
        w = [(self.semh[s], v) for (s, v) in wait if v > 0]
        ih = self.semh[inc]

        def run(e, w=w, out=out, in_=in_, ih=ih):
            for sh, v in w:
                e.wait_ge(sh, v)
            e.dma_start(out=out, in_=in_).then_inc(ih, 16)

        self.ops[eng].append(run)
        return post

    def val(self, sem):
        return self.cnt[sem]

    def wfree(self, key):
        return list(self.free.get(key, ()))

    def setfree(self, key, *tokens):
        self.free[key] = [t for t in tokens if t is not None]

    def barrier(self):
        snap = [(s, self.cnt[s]) for s in self.cnt]
        for eng in ENGINES:
            self.emit(eng, fn=None, wait=snap)


def build_program(upto="full"):
    nc = bass.Bass("TRN2", target_bir_lowering=False)
    ap = lambda t: t.ap()

    x0_d = ap(nc.dram_tensor("x0", [NT, 128, D], F32, kind="ExternalInput"))
    w1_d = ap(nc.dram_tensor("w1b", [E, 4, 128, 4, KD, 128], F32, kind="ExternalInput"))
    w2_d = ap(nc.dram_tensor("w2b", [E, 128, KD, FH], F32, kind="ExternalInput"))
    wg_d = ap(nc.dram_tensor("wgb", [128, KD, E], F32, kind="ExternalInput"))
    bg_d = ap(nc.dram_tensor("bgb", [1, E], F32, kind="ExternalInput"))
    wq_d = ap(nc.dram_tensor("wqb", [128, KD, D], F32, kind="ExternalInput"))
    wdkv_d = ap(nc.dram_tensor("wdkvb", [128, KD, L], F32, kind="ExternalInput"))
    wuk_d = ap(nc.dram_tensor("wukb", [128, KL, D], F32, kind="ExternalInput"))
    wuv_d = ap(nc.dram_tensor("wuvb", [128, KL, D], F32, kind="ExternalInput"))
    wo_d = ap(nc.dram_tensor("wob", [128, KD, D], F32, kind="ExternalInput"))
    gam_d = ap(nc.dram_tensor("gamb", [1, L], F32, kind="ExternalInput"))
    bet_d = ap(nc.dram_tensor("betb", [1, L], F32, kind="ExternalInput"))
    invf_d = ap(nc.dram_tensor("invfb", [1, 32], F32, kind="ExternalInput"))

    y_d = ap(nc.dram_tensor("y", [NT, 128, D], F32, kind="ExternalOutput"))
    xs1_d = ap(nc.dram_tensor("xs1", [NT, 128, D], F32, kind="Internal"))
    xs2_d = ap(nc.dram_tensor("xs2", [NT, 128, D], F32, kind="Internal"))
    t1s_d = ap(nc.dram_tensor("t1s", [128, KD, T], BF16, kind="Internal"))
    t2s_d = ap(nc.dram_tensor("t2s", [128, KD, T], BF16, kind="Internal"))
    t3s_d = ap(nc.dram_tensor("t3s", [128, KD, T], BF16, kind="Internal"))
    qscr_d = ap(nc.dram_tensor("qscr", [128, KD, T], BF16, kind="Internal"))
    kscr_d = ap(nc.dram_tensor("kscr", [128, KD, T], BF16, kind="Internal"))
    aos_d = ap(nc.dram_tensor("aos", [128, KD, T], BF16, kind="Internal"))

    from contextlib import ExitStack
    with ExitStack() as stack:
        def SBF(st):
            def f(name, shape, dtype=F32):
                return st.enter_context(nc.sbuf_tensor(name, shape, dtype))
            return f

        sb = SBF(stack)

        def psumt(name, shape, dtype=F32):
            return stack.enter_context(nc.psum_tensor(name, shape, dtype))

        xin = sb("xin", [128, 2, D], F32)
        x1b = sb("x1b", [128, 2, D], F32)
        t2b = sb("t2b", [128, 2, D], BF16)
        latnT = sb("latnT", [128, KL, T], BF16)
        qt = sb("qt", [128, D], F32)
        qkb = sb("qkb", [128, D], BF16)
        rtmp = None
        tstage = sb("tstage", [128, 2, KD, 128], BF16)
        latb = sb("latb", [128, 2, L], F32)
        latn_t = sb("latn_t", [128, L], BF16)
        ceb1 = sb("ceb1", [128, NT, E], F32)
        ceb3 = sb("ceb3", [128, NT, E], F32)
        logit1 = sb("logit1", [128, NT, E], F32)
        logit3 = sb("logit3", [128, NT, E], F32)
        gp = sb("gp", [128, 3, E], F32)
        statb = sb("statb", [128, 16], F32)
        sqb = sb("sqb", [128, D], F32)
        wgs = sb("wgs", [128, KD, E], BF16)
        bgbc = sb("bgbc", [128, E], F32)
        wdkvs = sb("wdkvs", [128, KD, L], BF16)
        wuks = sb("wuks", [128, KL, D], BF16)
        gbb = sb("gbb", [128, 2, L], F32)
        invfs = sb("invfs", [1, 32], F32)
        posr = sb("posr", [1, 128], F32)
        onesr = sb("onesr", [1, 128], F32)
        eyeb = sb("eyeb", [128, 128], BF16)
        eyef = sb("eyef", [128, 128], F32)
        iot = sb("iot", [128, 512], F32)
        angb = sb("angb", [128, 2, 32], F32)
        epsb = sb("epsb", [128, 1], F32)

        PB0 = psumt("PB0", [128, 1024], F32)
        PB1 = psumt("PB1", [128, 1024], F32)
        PB2 = psumt("PB2", [128, 1024], F32)
        PS6 = psumt("PS6", [128, 512], F32)
        PT = psumt("PT", [128, 512], BF16)
        Pp = [PB0[:, 0:512], PB0[:, 512:1024], PB1[:, 0:512], PB1[:, 512:1024],
              PB2[:, 0:512], PB2[:, 512:1024], PS6[:]]

        pb = PB(nc, stack)
        for s in ("pe", "dve", "act", "pool", "dx", "dxs", "dw", "dst"):
            pb.sem(s)
        EMIT, DMA, V, WF, SF = pb.emit, pb.dma, pb.val, pb.wfree, pb.setfree
        ESEM = {"vector": "dve", "scalar": "act", "tensor": "pe", "gpsimd": "pool"}

        def CH(eng, fn, wait=()):
            sm = ESEM[eng]
            return EMIT(eng, fn, wait=list(wait) + [(sm, V(sm))], inc=sm)

        DMA("sync", invfs[:], invf_d, inc="dx")
        DMA("sync", gbb[0:1, 0, :], gam_d, inc="dx")
        DMA("sync", gbb[0:1, 1, :], bet_d, inc="dx")
        DMA("sync", bgbc[0:1, :], bg_d, inc="dx")
        ld_const = V("dx")
        EMIT("gpsimd", lambda e: e.iota(iot[:], pattern=[[1, 512]], channel_multiplier=-1,
                                        allow_small_or_imprecise_dtypes=True), inc="pool")
        g_iota = V("pool")
        EMIT("vector", lambda e: e.memset(onesr[:], 1.0), inc="dve")
        EMIT("vector", lambda e: e.memset(epsb[:], EPS), inc="dve")
        EMIT("vector", lambda e: e.tensor_single_scalar(
            eyeb[:], iot[:, 0:128], 0.0, op=ALU.is_equal),
            wait=[("pool", g_iota)], inc="dve")
        EMIT("vector", lambda e: e.tensor_single_scalar(
            eyef[:], iot[:, 0:128], 0.0, op=ALU.is_equal), inc="dve")
        v_const = V("dve")

        for dst, src, n in ((gbb[:, 0, :], gbb[0:1, 0, :], L),
                            (gbb[:, 1, :], gbb[0:1, 1, :], L),
                            (bgbc[:, :], bgbc[0:1, :], E)):
            EMIT("tensor", lambda e, src=src, n=n: e.matmul(
                Pp[6][:, 0:n], onesr[0:1, :], src, start=True, stop=True),
                wait=[("dx", ld_const), ("dve", v_const)] + WF(("psum", 6)), inc="pe")
            pv = V("pe")
            EMIT("vector", lambda e, dst=dst, n=n: e.tensor_copy(dst, Pp[6][:, 0:n]),
                 wait=[("pe", pv)], inc="dve")
            SF(("psum", 6), ("dve", V("dve")))

        DMA("gpsimd", wgs[:], wg_d, inc="dw")
        DMA("gpsimd", wdkvs[:], wdkv_d, inc="dw")
        DMA("gpsimd", wuks[:], wuk_d, inc="dw")
        ld_small = V("dw")

        def ln_normalize(src_f32, out_bf16, setid, wait0=()):
            sc = 4 * setid
            EMIT("vector", lambda e: e.tensor_reduce(
                statb[:, sc:sc + 1], src_f32, axis=AX.X, op=ALU.add),
                wait=list(wait0) + WF(("lnstat", setid)), inc="dve")
            EMIT("vector", lambda e: e.tensor_tensor(sqb[:], src_f32, src_f32,
                                                     op=ALU.mult), inc="dve")
            CH("vector", lambda e: e.tensor_reduce(
                statb[:, sc + 1:sc + 2], sqb[:], axis=AX.X, op=ALU.add))
            CH("vector", lambda e: e.tensor_scalar_mul(
                statb[:, sc:sc + 1], statb[:, sc:sc + 1], 1.0 / D))
            CH("vector", lambda e: e.tensor_tensor(
                statb[:, sc + 2:sc + 3], statb[:, sc:sc + 1], statb[:, sc:sc + 1],
                op=ALU.mult))
            v_var = CH("vector", lambda e: e.scalar_tensor_tensor(
                statb[:, sc + 2:sc + 3], statb[:, sc + 1:sc + 2], 1.0 / D,
                statb[:, sc + 2:sc + 3], op0=ALU.mult, op1=ALU.subtract))
            a_sig = CH("scalar", lambda e: e.activation(
                statb[:, sc + 2:sc + 3], statb[:, sc + 2:sc + 3], AF.Sqrt, bias=epsb[:]),
                wait=[("dve", v_var)])
            CH("vector", lambda e: e.reciprocal(
                statb[:, sc + 3:sc + 4], statb[:, sc + 2:sc + 3]),
                wait=[("act", a_sig)])
            CH("vector", lambda e: e.tensor_tensor(
                statb[:, sc:sc + 1], statb[:, sc:sc + 1], statb[:, sc + 3:sc + 4],
                op=ALU.mult))
            v_nmr = CH("vector", lambda e: e.tensor_scalar_mul(
                statb[:, sc:sc + 1], statb[:, sc:sc + 1], -1.0))
            a_out = CH("scalar", lambda e: e.activation(
                out_bf16, src_f32, AF.Identity, bias=statb[:, sc:sc + 1],
                scale=statb[:, sc + 3:sc + 4]),
                wait=[("dve", v_nmr)])
            SF(("lnstat", setid), ("act", a_out))
            return a_out

        def transpose8(src_bf16_tile, dst_fn, wait_pe=(), wait_dve=(), kset=None):
            kset = list(range(KD)) if kset is None else list(kset)
            first = True
            for b0 in range(0, len(kset), 4):
                grp = kset[b0:b0 + 4]
                for i, k in enumerate(grp):
                    wl = WF("PT") if i == 0 else []
                    if first and i == 0:
                        wl = wl + list(wait_pe)
                    EMIT("tensor", lambda e, k=k, i=i: e.transpose(
                        PT[:, 128 * i:128 * (i + 1)],
                        src_bf16_tile[:, 128 * k:128 * (k + 1)], eyeb[:]),
                        wait=wl, inc=("pe" if i == len(grp) - 1 else None))
                first = False
                p_tp = V("pe")
                wl = [("pe", p_tp)] + (list(wait_dve) if b0 == 0 else [])
                for i, k in enumerate(grp):
                    EMIT("vector", lambda e, k=k, i=i: e.tensor_copy(
                        dst_fn(k), PT[:, 128 * i:128 * (i + 1)]),
                        wait=wl if i == 0 else (), inc="dve")
                SF("PT", ("dve", V("dve")))
            return V("dve"), V("pe")

        def gating_logits(lhsT_fn, dst_col, pe_wait):
            for k in range(KD):
                wl = []
                if k == 0:
                    wl = [("dw", ld_small)] + list(pe_wait) + WF(("psum", 6))
                EMIT("tensor", lambda e, k=k: e.matmul(
                    Pp[6][:, 0:E], lhsT_fn(k), wgs[:, k, :],
                    start=(k == 0), stop=(k == KD - 1)),
                    wait=wl, inc=("pe" if k == KD - 1 else None))
            pv = V("pe")
            EMIT("vector", lambda e, dst_col=dst_col: e.tensor_copy(dst_col, Pp[6][:, 0:E]),
                 wait=[("pe", pv)], inc="dve")
            SF(("psum", 6), ("dve", V("dve")))
            return V("dve")

        def gating_epilogue(logits, ceb, tile):
            lg = logits[:, tile, :]
            EMIT("vector", lambda e, lg=lg: e.tensor_tensor(lg, lg, bgbc[:, :],
                                                            op=ALU.add),
                 wait=WF("gstat"), inc="dve")
            v_nm = CH("vector", lambda e, lg=lg: e.tensor_reduce(
                statb[:, 8:9], lg, axis=AX.X, op=ALU.max, negate=True))
            a_p = CH("scalar", lambda e, lg=lg: e.activation(
                gp[:, 0, :], lg, AF.Exp, bias=statb[:, 8:9]),
                wait=[("dve", v_nm)])
            CH("vector", lambda e: e.tensor_reduce(
                statb[:, 9:10], gp[:, 0, :], axis=AX.X, op=ALU.max),
                wait=[("act", a_p)])
            CH("vector", lambda e: e.tensor_scalar(
                gp[:, 1, :], gp[:, 0, :], statb[:, 9:10], None, op0=ALU.is_equal))
            CH("vector", lambda e: e.scalar_tensor_tensor(
                gp[:, 1, :], gp[:, 1, :], -2.0, gp[:, 0, :], op0=ALU.mult, op1=ALU.add))
            CH("vector", lambda e: e.tensor_reduce(
                statb[:, 10:11], gp[:, 1, :], axis=AX.X, op=ALU.max))
            CH("vector", lambda e: e.tensor_tensor(
                statb[:, 11:12], statb[:, 9:10], statb[:, 10:11], op=ALU.add))
            CH("vector", lambda e: e.reciprocal(statb[:, 11:12], statb[:, 11:12]))
            CH("vector", lambda e: e.tensor_scalar(
                gp[:, 2, :], gp[:, 0, :], statb[:, 10:11], None, op0=ALU.is_ge))
            CH("vector", lambda e: e.tensor_tensor(
                gp[:, 0, :], gp[:, 0, :], gp[:, 2, :], op=ALU.mult))
            v_last = CH("vector", lambda e, ceb=ceb, tile=tile: e.tensor_scalar(
                ceb[:, tile, :], gp[:, 0, :], statb[:, 11:12], 0.5,
                op0=ALU.mult, op1=ALU.mult))
            SF("gstat", ("dve", v_last), ("act", a_p))
            return v_last

        def lat_ln(a_lat):
            EMIT("vector", lambda e: e.tensor_reduce(
                statb[:, 12:13], latb[:, 0, :], axis=AX.X, op=ALU.add),
                wait=[("act", a_lat)] + WF("latstat"), inc="dve")
            EMIT("vector", lambda e: e.tensor_tensor(
                sqb[:, 0:L], latb[:, 0, :], latb[:, 0, :], op=ALU.mult), inc="dve")
            CH("vector", lambda e: e.tensor_reduce(
                statb[:, 13:14], sqb[:, 0:L], axis=AX.X, op=ALU.add))
            CH("vector", lambda e: e.tensor_scalar_mul(
                statb[:, 12:13], statb[:, 12:13], 1.0 / L))
            CH("vector", lambda e: e.tensor_tensor(
                statb[:, 14:15], statb[:, 12:13], statb[:, 12:13], op=ALU.mult))
            v_var = CH("vector", lambda e: e.scalar_tensor_tensor(
                statb[:, 14:15], statb[:, 13:14], 1.0 / L, statb[:, 14:15],
                op0=ALU.mult, op1=ALU.subtract))
            a_sig = CH("scalar", lambda e: e.activation(
                statb[:, 14:15], statb[:, 14:15], AF.Sqrt, bias=epsb[:]),
                wait=[("dve", v_var)])
            CH("vector", lambda e: e.reciprocal(statb[:, 15:16], statb[:, 14:15]),
                wait=[("act", a_sig)])
            CH("vector", lambda e: e.tensor_tensor(
                statb[:, 12:13], statb[:, 12:13], statb[:, 15:16], op=ALU.mult))
            v_nmr = CH("vector", lambda e: e.tensor_scalar_mul(
                statb[:, 12:13], statb[:, 12:13], -1.0))
            a_z = CH("scalar", lambda e: e.activation(
                latb[:, 1, :], latb[:, 0, :], AF.Identity, bias=statb[:, 12:13],
                scale=statb[:, 15:16]), wait=[("dve", v_nmr)])
            SF("latstat", ("act", a_z))
            CH("vector", lambda e: e.tensor_tensor(
                latb[:, 1, :], latb[:, 1, :], gbb[:, 0, :], op=ALU.mult),
                wait=[("act", a_z)] + WF("latn_t"))
            v_out = CH("vector", lambda e: e.tensor_tensor(
                latn_t[:], latb[:, 1, :], gbb[:, 1, :], op=ALU.add))
            return v_out

        def resid_tile(g, base_load_d, tscr_d, logits, xs_store_d, have_delta,
                       delta_src=None, psum_pair=None):
            s = g % 2
            pp = g % 2
            ld = DMA("sync", xin[:, s, :], base_load_d[g], wait=WF(("xin", s)),
                     inc="dx")
            if have_delta == "buf":
                tt = g % CT
                EMIT("vector", lambda e, s=s, tt=tt, pp=pp: e.tensor_tensor(
                    x1b[:, pp, :], xin[:, s, :], delta_src[:, tt, :], op=ALU.add),
                    wait=[("dx", ld)] + WF(("x1b", pp)), inc="dve")
            elif have_delta == "psum":
                pA, pB, tok = psum_pair
                EMIT("vector", lambda e, s=s, pp=pp, pA=pA: e.tensor_tensor(
                    x1b[:, pp, 0:512], xin[:, s, 0:512], pA[:], op=ALU.add),
                    wait=[("dx", ld)] + WF(("x1b", pp)) + list(tok), inc="dve")
                EMIT("vector", lambda e, s=s, pp=pp, pB=pB: e.tensor_tensor(
                    x1b[:, pp, 512:1024], xin[:, s, 512:1024], pB[:], op=ALU.add),
                    inc="dve")
            v_x = V("dve")
            SF(("xin", s), ("dve", v_x))
            st_x = DMA("sync", xs_store_d[g], x1b[:, pp, :], wait=[("dve", v_x)],
                       inc="dst")
            a_t = ln_normalize(x1b[:, pp, :], t2b[:, pp, :], pp, wait0=WF(("t2b", pp)))
            SF(("x1b", pp), ("act", a_t), ("dst", st_x))
            sg = g % 2
            v_tp, p_tp = transpose8(
                t2b[:, pp, :], lambda k, sg=sg: tstage[:, sg, k, :],
                wait_pe=[("act", a_t)], wait_dve=WF(("tstage", sg)))
            SF(("t2b", pp), ("pe", p_tp))
            st_t = DMA("sync", tscr_d[:, :, 128 * g:128 * (g + 1)], tstage[:, sg, :, :],
                       wait=[("dve", v_tp)], inc="dst")
            if logits is not None:
                pv = gating_logits(lambda k, sg=sg: tstage[:, sg, k, :],
                                   logits[:, g, :], pe_wait=[("dve", v_tp)])
                SF(("tstage", sg), ("dst", st_t), ("pe", V("pe")))
            else:
                SF(("tstage", sg), ("dst", st_t))
            return dict(v_x=v_x, st_x=st_x, st_t=st_t, v_tp=v_tp, sg=sg)

        w1_idx = [0]

        def moe_call(sc, ceb, ce_ready_c, chunk_epilogue, tsrc_d, t_stored_c):
            w1r, w2r, aT, sgbL, delta, t1c = (sc["w1r"], sc["w2r"], sc["aT"],
                                              sc["sgb"], sc["delta"], sc["t1c"])
            t1c_ld = {}

            def ld_chunk(c):
                if c >= NCH:
                    return
                t1c_ld[c] = DMA("scalar", t1c[:, c % 2, :, :],
                                tsrc_d[:, :, CHUNK * c:CHUNK * (c + 1)],
                                wait=WF(("t1c", c % 2)) + [("dst", t_stored_c[c])],
                                inc="dxs")

            groups = [(c, e_i, g) for c in range(NCH) for e_i in range(E)
                      for g in range(4)]
            w1_ld = {}

            def emit_group_dma(gi):
                if gi >= len(groups):
                    return
                c, e_i, g = groups[gi]
                slot = gi % 2
                ld = DMA("gpsimd", w1r[:, slot, :, :, :], w1_d[e_i, g],
                         wait=WF(("w1", slot)), inc="dw")
                w1_ld[gi] = (slot, ld)

            ld_chunk(0)
            ld_chunk(1)
            for gi0 in range(2):
                emit_group_dma(gi0)
            gidx = [0]
            for c in range(NCH):
                tc = t1c[:, c % 2, :, :]
                for e_i in range(E):
                    w2_slot = e_i % 2
                    w2_ld = DMA("gpsimd", w2r[:, w2_slot, :, :], w2_d[e_i],
                                wait=WF(("w2", w2_slot)), inc="dw")
                    for mh in range(8):
                        if mh % 2 == 0:
                            gslot, gld = w1_ld.pop(gidx[0])
                        r = mh % 2
                        tg, tu = 2 * r, 2 * r + 1
                        ld_g = ld_u = gld
                        for nb in range(2):
                            gpar = (mh * 2 + nb) % 2
                            pgp = Pp[gpar]
                            for k in range(KD):
                                wl = []
                                if k == 0:
                                    wl = ([("dw", ld_g), ("dxs", t1c_ld[c])]
                                          + WF(("psum", gpar)))
                                EMIT("tensor", lambda e, k=k, gslot=gslot, tg=tg,
                                     nb=nb, pgp=pgp, tc=tc: e.matmul(
                                         pgp[:], w1r[:, gslot, tg, k, :],
                                         tc[:, k, 512 * nb:512 * (nb + 1)],
                                         start=(k == 0), stop=(k == KD - 1)),
                                     wait=wl, inc=("pe" if k == KD - 1 else None))
                            p_g = V("pe")
                            for k in range(KD):
                                wl = []
                                if k == 0:
                                    wl = [("dw", ld_u)] + WF(("psum", 2))
                                EMIT("tensor", lambda e, k=k, gslot=gslot, tu=tu,
                                     nb=nb, tc=tc: e.matmul(
                                         Pp[2][:], w1r[:, gslot, tu, k, :],
                                         tc[:, k, 512 * nb:512 * (nb + 1)],
                                         start=(k == 0), stop=(k == KD - 1)),
                                     wait=wl, inc=("pe" if k == KD - 1 else None))
                            p_u = V("pe")
                            if nb == 1 and mh % 2 == 1:
                                SF(("w1", gslot), ("pe", p_u))
                            EMIT("scalar", lambda e, pgp=pgp, gpar=gpar: e.activation(
                                sgbL[:, gpar, :], pgp[:], AF.Silu),
                                wait=[("pe", p_g)] + WF(("sgb", gpar)), inc="act")
                            a_sg = V("act")
                            SF(("psum", gpar), ("act", a_sg))
                            EMIT("vector", lambda e, mh=mh, nb=nb, gpar=gpar, aT=aT:
                                 e.tensor_tensor(
                                     aT[:, mh, 512 * nb:512 * (nb + 1)],
                                     Pp[2][:], sgbL[:, gpar, :], op=ALU.mult),
                                 wait=[("pe", p_u), ("act", a_sg)], inc="dve")
                            v_am = V("dve")
                            SF(("psum", 2), ("dve", v_am))
                            SF(("sgb", gpar), ("dve", v_am))
                        if mh % 2 == 1:
                            emit_group_dma(gidx[0] + 2)
                            gidx[0] += 1
                    v_a = V("dve")
                    for tt in range(CT):
                        for nb in range(2):
                            pi2 = 3 + 2 * (tt % 2) + nb
                            pop = Pp[pi2]
                            for k in range(KD):
                                wl = []
                                if k == 0:
                                    wl = ([("dve", v_a), ("dw", w2_ld)]
                                          + WF(("psum", pi2)))
                                EMIT("tensor", lambda e, k=k, tt=tt, nb=nb, pop=pop,
                                     w2_slot=w2_slot, w2r=w2r, aT=aT: e.matmul(
                                         pop[:], aT[:, k, 128 * tt:128 * (tt + 1)],
                                         w2r[:, w2_slot, k, 512 * nb:512 * (nb + 1)],
                                         start=(k == 0), stop=(k == KD - 1)),
                                     wait=wl, inc=("pe" if k == KD - 1 else None))
                            p_o = V("pe")
                            if tt == CT - 1 and nb == 1:
                                SF(("w2", w2_slot), ("pe", p_o))
                            gtile = c * CT + tt
                            if e_i == 0:
                                EMIT("vector", lambda e, tt=tt, nb=nb, pop=pop,
                                     gtile=gtile, ceb=ceb, delta=delta: e.tensor_scalar(
                                         delta[:, tt, 512 * nb:512 * (nb + 1)], pop[:],
                                         ceb[:, gtile, 0:1], None, op0=ALU.mult),
                                     wait=[("pe", p_o),
                                           ("dve", ce_ready_c[c])],
                                     inc="dve")
                            else:
                                EMIT("vector", lambda e, tt=tt, nb=nb, pop=pop,
                                     gtile=gtile, e_i=e_i, ceb=ceb, delta=delta:
                                     e.scalar_tensor_tensor(
                                         delta[:, tt, 512 * nb:512 * (nb + 1)], pop[:],
                                         ceb[:, gtile, e_i:e_i + 1],
                                         delta[:, tt, 512 * nb:512 * (nb + 1)],
                                         op0=ALU.mult, op1=ALU.add),
                                     wait=[("pe", p_o)], inc="dve")
                            SF(("psum", pi2), ("dve", V("dve")))
                SF(("t1c", c % 2), ("pe", V("pe")))
                ld_chunk(c + 2)
                chunk_epilogue(c)

        def rope_tile(g, a_ready, sintok, costok):
            cosb = costok[:, g:g + 1, :].to_broadcast([128, 16, 32])
            sinb = sintok[:, g:g + 1, :].to_broadcast([128, 16, 32])
            x1a = qt[:].rearrange("p (h two d) -> p h two d", h=16, two=2)[:, :, 0, :]
            x2a = qt[:].rearrange("p (h two d) -> p h two d", h=16, two=2)[:, :, 1, :]
            o1 = qkb[:].rearrange("p (h two d) -> p h two d", h=16, two=2)[:, :, 0, :]
            o2 = qkb[:].rearrange("p (h two d) -> p h two d", h=16, two=2)[:, :, 1, :]
            r1 = rtmp[:, 0, :].rearrange("p (h d) -> p h d", h=16)
            r2 = rtmp[:, 1, :].rearrange("p (h d) -> p h d", h=16)
            r3 = rtmp[:, 2, :].rearrange("p (h d) -> p h d", h=16)
            r4 = rtmp[:, 3, :].rearrange("p (h d) -> p h d", h=16)
            EMIT("vector", lambda e: e.tensor_tensor(r1, x1a, cosb, op=ALU.mult),
                 wait=[("act", a_ready)] + WF("qkb"), inc="dve")
            EMIT("vector", lambda e: e.tensor_tensor(r2, x2a, sinb, op=ALU.mult),
                 inc="dve")
            EMIT("vector", lambda e: e.tensor_tensor(r3, x1a, sinb, op=ALU.mult),
                 inc="dve")
            v4 = EMIT("vector", lambda e: e.tensor_tensor(r4, x2a, cosb, op=ALU.mult),
                      inc="dve")
            EMIT("vector", lambda e: e.tensor_tensor(o1, r1, r2, op=ALU.subtract),
                 wait=[("dve", v4 - 2)], inc="dve")
            EMIT("vector", lambda e: e.tensor_tensor(o2, r3, r4, op=ALU.add),
                 wait=[("dve", v4)], inc="dve")
            return V("dve")

        with ExitStack() as scA:
            sba = SBF(scA)
            t1c = sba("t1c", [128, 2, KD, CHUNK], BF16)
            delta = sba("delta", [128, CT, D], F32)
            aT = sba("aT", [128, KD, CHUNK], BF16)
            w1r = sba("w1r", [128, 2, 4, KD, 128], BF16)
            w2r = sba("w2r", [128, 2, KD, FH], BF16)
            sgbA = sba("sgbA", [128, 2, 512], F32)
            scbufs = dict(w1r=w1r, w2r=w2r, aT=aT, sgb=sgbA, delta=delta, t1c=t1c)

            t1_stored_c = {}
            ce1_ready_c = {}
            for c0 in range(NCH):
                for tt in range(CT):
                    g = c0 * CT + tt
                    s = g % 2
                    pp = g % 2
                    ld = DMA("sync", xin[:, s, :], x0_d[g], wait=WF(("xin", s)),
                             inc="dx")
                    a_t1 = ln_normalize(xin[:, s, :], t2b[:, pp, :], pp,
                                        wait0=[("dx", ld)] + WF(("t2b", pp)))
                    SF(("xin", s), ("act", a_t1), ("dve", V("dve")))
                    sg = g % 2
                    v_tp, p_tp = transpose8(
                        t2b[:, pp, :], lambda k, sg=sg: tstage[:, sg, k, :],
                        wait_pe=[("act", a_t1)], wait_dve=WF(("tstage", sg)))
                    SF(("t2b", pp), ("pe", p_tp))
                    st_t = DMA("sync", t1s_d[:, :, 128 * g:128 * (g + 1)],
                               tstage[:, sg, :, :], wait=[("dve", v_tp)], inc="dst")
                    gating_logits(lambda k, sg=sg: tstage[:, sg, k, :],
                                  logit1[:, g, :], pe_wait=[("dve", v_tp)])
                    SF(("tstage", sg), ("dst", st_t), ("pe", V("pe")))
                t1_stored_c[c0] = V("dst")
                for tt in range(CT):
                    gating_epilogue(logit1, ceb1, c0 * CT + tt)
                ce1_ready_c[c0] = V("dve")

            def p1_epilogue(c):
                for tt in range(CT):
                    g = c * CT + tt
                    rt = resid_tile(g, x0_d, t2s_d, None, xs1_d, "buf",
                                    delta_src=delta)
                    sg = rt["sg"]
                    for k in range(KD):
                        wl = (WF(("psum", 2)) + [("dve", rt["v_tp"])]) if k == 0 else []
                        EMIT("tensor", lambda e, k=k, sg=sg: e.matmul(
                            Pp[2][:, 0:L], tstage[:, sg, k, :], wdkvs[:, k, :],
                            start=(k == 0), stop=(k == KD - 1)),
                            wait=wl, inc=("pe" if k == KD - 1 else None))
                    p_lat = V("pe")
                    SF(("tstage", sg), *(WF(("tstage", sg)) + [("pe", p_lat)]))
                    EMIT("scalar", lambda e: e.copy(latb[:, 0, :], Pp[2][:, 0:L]),
                         wait=[("pe", p_lat)] + WF("latb"), inc="act")
                    a_lat = V("act")
                    SF(("psum", 2), ("act", a_lat))
                    v_latn = lat_ln(a_lat)
                    SF("latb", ("dve", v_latn))
                    for kk in range(KL):
                        EMIT("tensor", lambda e, kk=kk: e.transpose(
                            PT[:, 0:128], latn_t[:, 128 * kk:128 * (kk + 1)], eyeb[:]),
                            wait=([("dve", v_latn)] if kk == 0 else []) + WF("PT"),
                            inc="pe")
                        p_tp = V("pe")
                        EMIT("vector", lambda e, kk=kk, g=g: e.tensor_copy(
                            latnT[:, kk, 128 * g:128 * (g + 1)], PT[:, 0:128]),
                            wait=[("pe", p_tp)], inc="dve")
                        SF("PT", ("dve", V("dve")))
                    SF("latn_t", ("pe", V("pe")))

            moe_call(scbufs, ceb1, ce1_ready_c, p1_epilogue, t1s_d, t1_stored_c)
            pb.barrier()
        a_end = {s: V(s) for s in ("pe", "dve", "act", "dst", "dx", "dw")}

        def dump_dram_f32(src_d):
            for g in range(NT):
                s = g % 2
                ld = DMA("sync", xin[:, s, :], src_d[g], wait=WF(("xin", s)), inc="dx")
                EMIT("vector", lambda e, s=s, g=g: e.tensor_copy(
                    x1b[:, g % 2, :], xin[:, s, :]),
                    wait=[("dx", ld)] + WF(("x1b", g % 2)), inc="dve")
                v_y = V("dve")
                SF(("xin", s), ("dve", v_y))
                st = DMA("sync", y_d[g], x1b[:, g % 2, :], wait=[("dve", v_y)],
                         inc="dst")
                SF(("x1b", g % 2), ("dst", st))
            EMIT("sync", fn=None, wait=[("dst", V("dst"))])

        def dump_dram_bf16(src_d):
            for g in range(NT):
                sg = g % 2
                ld = DMA("sync", tstage[:, sg, :, :], src_d[:, :, 128 * g:128 * (g + 1)],
                         wait=WF(("tstage", sg)), inc="dx")
                EMIT("vector", lambda e, sg=sg: e.tensor_copy(
                    x1b[:, sg, :], tstage[:, sg, :, :].rearrange("p k t -> p (k t)")),
                    wait=[("dx", ld)] + WF(("x1b", sg)), inc="dve")
                v_y = V("dve")
                SF(("tstage", sg), ("dve", v_y))
                st = DMA("sync", y_d[g], x1b[:, sg, :], wait=[("dve", v_y)], inc="dst")
                SF(("x1b", sg), ("dst", st))
            EMIT("sync", fn=None, wait=[("dst", V("dst"))])

        if upto not in ("x1", "t1", "t2", "ce"):
            with ExitStack() as scB:
                sbb = SBF(scB)
                rtmp = sbb("rtmp", [128, 4, 512], F32)
                t2c = sbb("t2c", [128, KD, CHUNK], BF16)
                wqs = sbb("wqs", [128, KD, D], BF16)
                wuvs = sbb("wuvs", [128, KL, D], BF16)
                wos = sbb("wos", [128, KD, D], BF16)
                sintok = sbb("sintok", [128, NT, 32], F32)
                costok = sbb("costok", [128, NT, 32], F32)
                maskb = sbb("maskb", [128, 4, 512], BF16)
                qhb = sbb("qhb", [64, 2, T], BF16)
                khb = sbb("khb", [64, 2, T], BF16)
                vhb = sbb("vhb", [128, 2, NT, 65], BF16)
                eTb = sbb("eTb", [128, 6, 512], BF16)
                mparts = sbb("mparts", [128, 8], F32)
                mrun = sbb("mrun", [128, 1], F32)
                mhsc = sbb("mhsc", [1, 4], F32)
                qsq = sbb("qsq", [64, T // 2], BF16)
                onesc = sbb("onesc", [64, 1], BF16)
                negm = sbb("negm", [128, 2], F32)
                rcrow = sbb("rcrow", [1, 512], F32)
                rbs = sbb("rbs", [64, 512], F32)
                atmp = sbb("atmp", [64, 512], BF16)
                aoc = sbb("aoc", [128, KD, 512], BF16)

                DMA("gpsimd", wqs[:], wq_d, inc="dw")
                DMA("gpsimd", wuvs[:], wuv_d, inc="dw")
                DMA("gpsimd", wos[:], wo_d, inc="dw")
                ld_bw = V("dw")

                for d in range(4):
                    EMIT("vector", lambda e, d=d: e.tensor_single_scalar(
                        maskb[:, d, :], iot[:], float(128 * d), op=ALU.is_ge),
                        inc="dve")
                for t in range(NT):
                    g_pos = CH("gpsimd", lambda e, t=t: e.iota(
                        posr[:], pattern=[[1, 128]], base=128 * t,
                        channel_multiplier=0, allow_small_or_imprecise_dtypes=True))
                    EMIT("tensor", lambda e: e.matmul(
                        Pp[6][:, 0:32], posr[0:1, :], invfs[0:1, :],
                        start=True, stop=True),
                        wait=[("pool", g_pos)] + WF(("psum", 6)), inc="pe")
                    pv = V("pe")
                    EMIT("vector", lambda e: e.tensor_copy(angb[:, 0, :], Pp[6][:, 0:32]),
                         wait=[("pe", pv)] + WF("angb"), inc="dve")
                    SF(("psum", 6), ("dve", V("dve")))
                    for dst, shift in ((sintok, 0.0), (costok, HALFPI)):
                        if shift:
                            CH("vector", lambda e: e.tensor_scalar_add(
                                angb[:, 0, :], angb[:, 0, :], shift))
                        CH("vector", lambda e: e.tensor_scalar(
                            angb[:, 1, :], angb[:, 0, :], INV2PI, MAGIC,
                            op0=ALU.mult, op1=ALU.add))
                        CH("vector", lambda e: e.tensor_scalar_sub(
                            angb[:, 1, :], angb[:, 1, :], MAGIC))
                        CH("vector", lambda e, dst=dst, t=t: e.scalar_tensor_tensor(
                            dst[:, t, :], angb[:, 1, :], -RC1, angb[:, 0, :],
                            op0=ALU.mult, op1=ALU.add))
                        v_ang = CH("vector", lambda e, dst=dst, t=t: e.scalar_tensor_tensor(
                            dst[:, t, :], angb[:, 1, :], -RC2, dst[:, t, :],
                            op0=ALU.mult, op1=ALU.add))
                    a_s1 = CH("scalar", lambda e, t=t: e.activation(
                        sintok[:, t, :], sintok[:, t, :], AF.Sin),
                        wait=[("dve", v_ang)])
                    a_s2 = CH("scalar", lambda e, t=t: e.activation(
                        costok[:, t, :], costok[:, t, :], AF.Sin))
                    SF("angb", ("act", a_s2))
                tabs_done_a = V("act")

                def qk_proj(g, lhsT_fn, nk, rhs_w, scr_d, extra_wait):
                    for nb in range(2):
                        pq = Pp[nb]
                        for k in range(nk):
                            wl = []
                            if k == 0:
                                wl = (list(extra_wait) + WF(("psum", nb))
                                      + [("dw", ld_bw)])
                            EMIT("tensor", lambda e, k=k, nb=nb, pq=pq: e.matmul(
                                pq[:], lhsT_fn(k),
                                rhs_w[:, k, 512 * nb:512 * (nb + 1)],
                                start=(k == 0), stop=(k == nk - 1)),
                                wait=wl, inc=("pe" if k == nk - 1 else None))
                        p_q = V("pe")
                        EMIT("scalar", lambda e, nb=nb, pq=pq: e.copy(
                            qt[:, 512 * nb:512 * (nb + 1)], pq[:]),
                            wait=[("pe", p_q)] + WF("qt"), inc="act")
                        SF(("psum", nb), ("act", V("act")))
                    a_qt = V("act")
                    v_rope = rope_tile(g, a_qt, sintok, costok)
                    SF("qt", ("dve", v_rope))
                    sg = g % 2
                    v_tp, p_tp = transpose8(
                        qkb[:], lambda k, sg=sg: tstage[:, sg, k, :],
                        wait_pe=[("dve", v_rope), ("act", tabs_done_a)],
                        wait_dve=WF(("tstage", sg)))
                    SF("qkb", ("pe", p_tp))
                    st = DMA("sync", scr_d[:, :, 128 * g:128 * (g + 1)],
                             tstage[:, sg, :, :], wait=[("dve", v_tp)], inc="dst")
                    SF(("tstage", sg), ("dst", st))
                    return st

                qk_stored = 0
                for c in range(NCH):
                    ld_t2 = DMA("scalar", t2c[:, :, :],
                                t2s_d[:, :, CHUNK * c:CHUNK * (c + 1)],
                                wait=WF("t2c"), inc="dxs")
                    for tt in range(CT):
                        g = c * CT + tt
                        qk_proj(g, lambda k, tt=tt: t2c[:, k, 128 * tt:128 * (tt + 1)],
                                KD, wqs, qscr_d, [("dxs", ld_t2)])
                        qk_stored = qk_proj(
                            g, lambda k, g=g: latnT[:, k, 128 * g:128 * (g + 1)],
                            KL, wuks, kscr_d, [])
                    SF("t2c", ("pe", V("pe")))

                PMh = PB2
                PAVs = [PS6, PB2[:, 512:1024]]
                SF(("pA",), *(WF(("psum", 0)) + WF(("psum", 1))))
                SF(("pB",), *(WF(("psum", 2)) + WF(("psum", 3))))
                SF(("pM", 0), *WF(("psum", 4)))
                SF(("pAV", 1), *WF(("psum", 5)))
                SF(("pAV", 0), *WF(("psum", 6)))
                EMIT("vector", lambda e: e.memset(vhb[:, 0, :, 64:65], 1.0), inc="dve")
                EMIT("vector", lambda e: e.memset(vhb[:, 1, :, 64:65], 1.0), inc="dve")
                EMIT("vector", lambda e: e.memset(onesc[:], 1.0), inc="dve")

                ld_q, ld_k = {}, {}

                def head_loads(hn):
                    if hn >= H:
                        return
                    hp = hn % 2
                    kd_h, off = hn // 2, 64 * (hn % 2)
                    ld_q[hn] = DMA("sync", qhb[:, hp, :],
                                   qscr_d[off:off + 64, kd_h, :],
                                   wait=WF(("qh", hp)) +
                                   ([("dst", qk_stored)] if hn < 2 else []), inc="dx")
                    ld_k[hn] = DMA("sync", khb[:, hp, :],
                                   kscr_d[off:off + 64, kd_h, :],
                                   wait=WF(("kh", hp)), inc="dx")

                pm_rr = [0]

                def emit_item(hn, item):
                    hp = hn % 2
                    kind = item[0]
                    par = 0
                    pmap = PMh[:, 0:512]
                    if kind == "V":
                        tt = item[1]
                        for kl in range(KL):
                            wl = WF(("pM", par)) if kl == 0 else []
                            EMIT("tensor", lambda e, kl=kl, tt=tt, hn=hn, pmap=pmap:
                                 e.matmul(
                                     pmap[:, 0:64],
                                     latnT[:, kl, 128 * tt:128 * (tt + 1)],
                                     wuvs[:, kl, 64 * hn:64 * (hn + 1)],
                                     start=(kl == 0), stop=(kl == KL - 1)),
                                 wait=wl, inc=("pe" if kl == KL - 1 else None))
                        p_v = V("pe")
                        EMIT("scalar", lambda e, tt=tt, hp=hp, pmap=pmap: e.copy(
                            vhb[:, hp, tt, 0:64], pmap[:, 0:64]),
                            wait=[("pe", p_v)] + WF(("vh", hp, tt)), inc="act")
                        SF(("pM", par), ("act", V("act")))
                    elif kind == "SQ":
                        which, hf = item[1], item[2]
                        src = qhb if which == "q" else khb
                        wl = ([("dx", (ld_q if which == "q" else ld_k)[hn])]
                              + WF("qsq"))
                        EMIT("vector", lambda e, src=src, hp=hp, hf=hf:
                             e.tensor_tensor(
                                 qsq[:], src[:, hp, 2048 * hf:2048 * (hf + 1)],
                                 src[:, hp, 2048 * hf:2048 * (hf + 1)], op=ALU.mult),
                             wait=wl, inc="dve")
                    elif kind == "NRM":
                        which, j = item[1], item[2]
                        v_sq = V("dve")
                        EMIT("tensor", lambda e, j=j, pmap=pmap: e.matmul(
                            pmap[0:1, 0:512], onesc[:],
                            qsq[:, 512 * (j % 4):512 * (j % 4 + 1)],
                            start=True, stop=True),
                            wait=[("dve", v_sq)] + WF(("pM", par)), inc="pe")
                        p_n = V("pe")
                        col = 0 if which == "q" else 1
                        EMIT("vector", lambda e, pmap=pmap: e.tensor_reduce(
                            mhsc[0:1, 2:3], pmap[0:1, 0:512], axis=AX.X, op=ALU.max),
                            wait=[("pe", p_n)], inc="dve")
                        SF(("pM", par), ("dve", V("dve")))
                        if j == 0:
                            CH("vector", lambda e, col=col: e.tensor_copy(
                                mhsc[0:1, col:col + 1], mhsc[0:1, 2:3]))
                        else:
                            CH("vector", lambda e, col=col: e.tensor_tensor(
                                mhsc[0:1, col:col + 1], mhsc[0:1, col:col + 1],
                                mhsc[0:1, 2:3], op=ALU.max))
                        if j % 4 == 3:
                            SF("qsq", ("pe", p_n))
                    elif kind == "MNB":
                        v_p = CH("vector", lambda e: e.tensor_tensor(
                            mhsc[0:1, 2:3], mhsc[0:1, 0:1], mhsc[0:1, 1:2],
                            op=ALU.mult))
                        a_sq = CH("scalar", lambda e: e.activation(
                            mhsc[0:1, 2:3], mhsc[0:1, 2:3], AF.Sqrt),
                            wait=[("dve", v_p)])
                        v_mh = CH("vector", lambda e: e.tensor_scalar_mul(
                            mhsc[0:1, 3:4], mhsc[0:1, 2:3], -0.125),
                            wait=[("act", a_sq)])
                        EMIT("tensor", lambda e, pmap=pmap: e.matmul(
                            pmap[:, 0:1], onesr[0:1, :], mhsc[0:1, 3:4],
                            start=True, stop=True),
                            wait=[("dve", v_mh)] + WF(("pM", 0)), inc="pe")
                        p_nb = V("pe")
                        EMIT("vector", lambda e, hp=hp, pmap=pmap: e.tensor_copy(
                            negm[:, hp:hp + 1], pmap[:, 0:1]),
                            wait=[("pe", p_nb)], inc="dve")
                        SF(("pM", par), ("dve", V("dve")))
                        SF(("negm", hp), ("dve", V("dve")))

                def make_items(hn):
                    items = []
                    if hn >= H:
                        return items
                    for tt in range(NT):
                        items.append(("V", tt))
                    for which in ("q", "k"):
                        for hf in range(2):
                            items.append(("SQ", which, hf))
                            for j in range(4 * hf, 4 * hf + 4):
                                items.append(("NRM", which, j))
                    items.append(("MNB",))
                    return items

                head_loads(0)
                head_loads(1)
                for it in make_items(0):
                    emit_item(0, it)
                for h in range(H):
                    hp = h % 2
                    kd_h, off = h // 2, 64 * (h % 2)
                    items = make_items(h + 1)
                    qi = [0]

                    def drip(n=3):
                        for _ in range(n):
                            if qi[0] < len(items):
                                emit_item(h + 1, items[qi[0]])
                                qi[0] += 1

                    v_negm = [t for t in WF(("negm", hp))]
                    for st_i in range(8):
                        jmax = 4 * (st_i + 1)
                        npairs = jmax // 2
                        PAV = PAVs[st_i % 2]
                        avkey = ("pAV", st_i % 2)
                        pend = None
                        for jp in range(npairs + 1):
                            if jp < npairs:
                                j0 = 2 * jp
                                ppar = jp % 2
                                PAx = PB0 if ppar == 0 else PB1
                                pkey = ("pA",) if ppar == 0 else ("pB",)
                                epar = jp % 3
                                for jj in (j0, j0 + 1):
                                    wl = []
                                    if jj == j0:
                                        wl = WF(pkey)
                                    EMIT("tensor", lambda e, jj=jj, st_i=st_i, hp=hp,
                                         PAx=PAx: e.matmul(
                                             PAx[:, 512 * (jj % 2):512 * (jj % 2 + 1)],
                                             khb[:, hp, 128 * jj:128 * (jj + 1)],
                                             qhb[:, hp, 512 * st_i:512 * (st_i + 1)],
                                             start=True, stop=True),
                                         wait=wl, inc=("pe" if jj > j0 else None))
                                p_s = V("pe")
                                ew = ([("pe", p_s)] + WF(("eTp", epar))
                                      + list(v_negm))
                                EMIT("scalar", lambda e, epar=epar, hp=hp, PAx=PAx:
                                     e.activation(
                                         eTb[:, 2 * epar:2 * epar + 2, :].rearrange(
                                             "p a t -> p (a t)"),
                                         PAx[:], AF.Exp,
                                         bias=negm[:, hp:hp + 1], scale=0.125),
                                     wait=ew, inc="act")
                                a_e = V("act")
                                SF(pkey, ("act", a_e))
                                toks = {}
                                for jj in (j0, j0 + 1):
                                    di = jj - 4 * st_i
                                    esl = 2 * epar + (jj - j0)
                                    if di >= 0:
                                        EMIT("vector", lambda e, esl=esl, di=di:
                                             e.tensor_tensor(
                                                 eTb[:, esl, :], eTb[:, esl, :],
                                                 maskb[:, di, :], op=ALU.mult),
                                             wait=[("act", a_e)], inc="dve")
                                        toks[jj] = ("dve", V("dve"))
                                    else:
                                        toks[jj] = ("act", a_e)
                                drip(1)
                            if pend is not None:
                                pj0, pepar, ptoks = pend
                                for jj in (pj0, pj0 + 1):
                                    esl = 2 * pepar + (jj - pj0)
                                    wl = [ptoks[jj]]
                                    if jj == 0:
                                        wl += list(avkey and WF(avkey))
                                    EMIT("tensor", lambda e, jj=jj, hp=hp, esl=esl,
                                         jmax=jmax: e.matmul(
                                             PAV[0:65, :], vhb[:, hp, jj, 0:65],
                                             eTb[:, esl, :],
                                             start=(jj == 0), stop=(jj == jmax - 1)),
                                         wait=wl, inc="pe")
                                SF(("eTp", pepar), ("pe", V("pe")))
                                drip(1)
                            if jp < npairs:
                                pend = (j0, epar, toks)
                        p_av = V("pe")
                        EMIT("vector", lambda e: e.tensor_scalar_max(
                            rcrow[:], PAV[64:65, :], 1.0e-30),
                            wait=[("pe", p_av)], inc="dve")
                        v_rc = CH("vector", lambda e: e.reciprocal(
                            rcrow[:], rcrow[:]))
                        pmap3 = PMh[:, 0:512]
                        EMIT("tensor", lambda e, pmap3=pmap3: e.matmul(
                            pmap3[0:64, :], onesr[0:1, 0:64], rcrow[0:1, :],
                            start=True, stop=True),
                            wait=[("dve", v_rc)] + WF(("pM", 0)), inc="pe")
                        p_rb = V("pe")
                        EMIT("vector", lambda e, pmap3=pmap3: e.tensor_copy(
                            rbs[:], pmap3[0:64, :]),
                            wait=[("pe", p_rb)] + WF("rbs"), inc="dve")
                        SF(("pM", 0), ("dve", V("dve")))
                        v_at = CH("vector", lambda e: e.tensor_tensor(
                            atmp[:], PAV[0:64, :], rbs[:], op=ALU.mult),
                            wait=WF("atmp"))
                        SF(avkey, ("dve", v_at))
                        SF("rbs", ("dve", v_at))
                        st_ao = DMA("sync",
                                    aos_d[off:off + 64, kd_h,
                                          512 * st_i:512 * (st_i + 1)],
                                    atmp[:], wait=[("dve", v_at)], inc="dst")
                        SF("atmp", ("dst", st_ao))
                    while qi[0] < len(items):
                        emit_item(h + 1, items[qi[0]])
                        qi[0] += 1
                    p_heads_end = V("pe")
                    SF(("qh", hp), ("pe", p_heads_end))
                    SF(("kh", hp), ("pe", p_heads_end))
                    for tt in range(NT):
                        SF(("vh", hp, tt), ("pe", p_heads_end))
                    head_loads(h + 2)
                SF(("psum", 0), *WF(("pA",)))
                SF(("psum", 1), *WF(("pA",)))
                SF(("psum", 2), *WF(("pB",)))
                SF(("psum", 3), *WF(("pB",)))
                SF(("psum", 4), *WF(("pM", 0)))
                SF(("psum", 5), *WF(("pAV", 1)))
                SF(("psum", 6), *WF(("pAV", 0)))
                aos_stored = V("dst")

                for st_i in range(8):
                    ld_ao = DMA("scalar", aoc[:, :, :],
                                aos_d[:, :, 512 * st_i:512 * (st_i + 1)],
                                wait=WF("aoc") +
                                ([("dst", aos_stored)] if st_i == 0 else []),
                                inc="dxs")
                    for tl in range(4):
                        g = 4 * st_i + tl
                        p_o = None
                        for nb in range(2):
                            pio = 2 * (tl % 2) + nb
                            pop = Pp[pio]
                            for k in range(KD):
                                wl = []
                                if k == 0:
                                    wl = [("dxs", ld_ao)] + WF(("psum", pio))
                                EMIT("tensor", lambda e, k=k, tl=tl, nb=nb, pop=pop:
                                     e.matmul(
                                         pop[:], aoc[:, k, 128 * tl:128 * (tl + 1)],
                                         wos[:, k, 512 * nb:512 * (nb + 1)],
                                         start=(k == 0), stop=(k == KD - 1)),
                                     wait=wl, inc=("pe" if k == KD - 1 else None))
                            p_o = V("pe")
                        pA, pB = Pp[2 * (tl % 2)], Pp[2 * (tl % 2) + 1]
                        rt = resid_tile(g, xs1_d, t3s_d, logit3, xs2_d, "psum",
                                        psum_pair=(pA, pB, [("pe", p_o)]))
                        SF(("psum", 2 * (tl % 2)), ("dve", rt["v_x"]))
                        SF(("psum", 2 * (tl % 2) + 1), ("dve", rt["v_x"]))
                    SF("aoc", ("pe", V("pe")))
                t3_stored = V("dst")
                for g in range(NT):
                    gating_epilogue(logit3, ceb3, g)
                v_ce3 = V("dve")
                t3_stored_c = {c: t3_stored for c in range(NCH)}
                ce3_ready_c = {c: v_ce3 for c in range(NCH)}
                pb.barrier()

            if upto == "x2":
                dump_dram_f32(xs2_d)
            else:
                with ExitStack() as scC:
                    sbc = SBF(scC)
                    t3c = sbc("t3c", [128, 2, KD, CHUNK], BF16)
                    deltaC = sbc("deltaC", [128, CT, D], F32)
                    aTC = sbc("aTC", [128, KD, CHUNK], BF16)
                    w1rC = sbc("w1rC", [128, 2, 4, KD, 128], BF16)
                    w2rC = sbc("w2rC", [128, 2, KD, FH], BF16)
                    sgbC = sbc("sgbC", [128, 2, 512], F32)
                    scbufsC = dict(w1r=w1rC, w2r=w2rC, aT=aTC, sgb=sgbC,
                                   delta=deltaC, t1c=t3c)

                    def p3_epilogue(c):
                        for tt in range(CT):
                            g = c * CT + tt
                            sx = g % 2
                            ld = DMA("sync", xin[:, sx, :], xs2_d[g],
                                     wait=WF(("xin", sx)), inc="dx")
                            EMIT("vector", lambda e, sx=sx, tt=tt: e.tensor_tensor(
                                x1b[:, sx, :], xin[:, sx, :], deltaC[:, tt, :],
                                op=ALU.add),
                                wait=[("dx", ld)] + WF(("x1b", sx)), inc="dve")
                            v_y = V("dve")
                            SF(("xin", sx), ("dve", v_y))
                            st = DMA("sync", y_d[g], x1b[:, sx, :],
                                     wait=[("dve", v_y)], inc="dst")
                            SF(("x1b", sx), ("dst", st))

                    moe_call(scbufsC, ceb3, ce3_ready_c, p3_epilogue, t3s_d, t3_stored_c)
                    EMIT("sync", fn=None, wait=[("dst", V("dst"))])

        if upto == "x1":
            dump_dram_f32(xs1_d)
        elif upto == "t1":
            dump_dram_bf16(t1s_d)
        elif upto == "t2":
            dump_dram_bf16(t2s_d)
        elif upto == "ce":
            for g in range(NT):
                EMIT("vector", lambda e, g=g: e.tensor_copy(
                    x1b[:, g % 2, 0:E], ceb1[:, g, :]),
                    wait=WF(("x1b", g % 2)), inc="dve")
                EMIT("vector", lambda e, g=g: e.memset(x1b[:, g % 2, E:D], 0.0),
                     inc="dve")
                v_y = V("dve")
                st = DMA("sync", y_d[g], x1b[:, g % 2, :], wait=[("dve", v_y)],
                         inc="dst")
                SF(("x1b", g % 2), ("dst", st))
            EMIT("sync", fn=None, wait=[("dst", V("dst"))])

        with nc.Block() as block:
            for eng in ENGINES:
                def make(eng):
                    def body(e):
                        for f in pb.ops[eng]:
                            f(e)
                    return body
                getattr(block, eng)(make(eng))

    return nc


def _prep_weights(i):
    f32 = np.float32

    def pmaj(w, kt, x):
        return np.ascontiguousarray(
            np.asarray(w, dtype=f32).reshape(kt, 128, x).transpose(1, 0, 2))

    w1 = np.ascontiguousarray(i["L0_ffn_w1"], dtype=f32)
    w1b = np.ascontiguousarray(
        w1.reshape(E, 2, 4, 2, 128, KD, 128).transpose(0, 2, 6, 3, 1, 5, 4)
        .reshape(E, 4, 128, 4, KD, 128))
    w2 = np.ascontiguousarray(i["L0_ffn_w2"], dtype=f32)
    w2b = np.ascontiguousarray(
        w2.transpose(0, 2, 1).reshape(E, KD, 128, FH).transpose(0, 2, 1, 3))
    wgb = pmaj(np.asarray(i["L0_ffn_Wg"], dtype=f32).T, KD, E)
    bgb = np.ascontiguousarray(np.asarray(i["L0_ffn_bg"], dtype=f32).reshape(1, E))
    wqb = pmaj(i["L0_attn_W_q"], KD, D)
    wdkvb = pmaj(i["L0_attn_W_dkv"], KD, L)
    wukb = pmaj(i["L0_attn_W_uk"], KL, D)
    wuvb = pmaj(i["L0_attn_W_uv"], KL, D)
    wob = pmaj(i["L0_attn_W_o"], KD, D)
    gamb = np.ascontiguousarray(np.asarray(i["L0_attn_ln_gamma"], dtype=f32).reshape(1, L))
    betb = np.ascontiguousarray(np.asarray(i["L0_attn_ln_beta"], dtype=f32).reshape(1, L))
    invfb = (1.0 / (10000.0 ** (np.arange(32, dtype=np.float64) / 32))).astype(
        f32).reshape(1, 32)
    return dict(w1b=w1b, w2b=w2b, wgb=wgb, bgb=bgb, wqb=wqb, wdkvb=wdkvb,
                wukb=wukb, wuvb=wuvb, wob=wob, gamb=gamb, betb=betb, invfb=invfb)


def kernel(_upto="full", _cores=8, **inputs):
    x = np.ascontiguousarray(np.asarray(inputs["L0_attn_x"]), dtype=np.float32)
    wb = _prep_weights({k: np.asarray(v) for k, v in inputs.items()})
    nc = build_program(upto=_upto)
    in_maps = []
    for c in range(_cores):
        m = dict(wb)
        m["x0"] = np.ascontiguousarray(x[c].reshape(NT, 128, D))
        in_maps.append(m)
    res = run_bass_kernel_spmd(nc, in_maps, core_ids=list(range(_cores)))
    return np.stack([r["y"].reshape(T, D) for r in res.results])


# version 07
# speedup vs baseline: 1.0408x; 1.0408x over previous
import numpy as np
import ml_dtypes
import concourse.bass as bass
import concourse.mybir as mybir
from concourse.bass_utils import run_bass_kernel_spmd

F32, BF16 = mybir.dt.float32, mybir.dt.bfloat16
AF = mybir.ActivationFunctionType
ALU = mybir.AluOpType
AX = mybir.AxisListType

T, D, H, DH, L, E, FH = 4096, 1024, 16, 64, 256, 8, 1024
NT = T // 128
CHUNK = 1024
NCH = T // CHUNK
CT = CHUNK // 128
KD = D // 128
KL = L // 128
EPS = 1e-5
INV2PI = float(np.float32(1.0 / (2 * np.pi)))
MAGIC = float(np.float32(3 * 2.0 ** 22))
RC1 = float(np.float32(6.28125))
RC2 = float(np.float32(2 * np.pi - 6.28125))
HALFPI = float(np.float32(np.pi / 2))

ENGINES = ("tensor", "vector", "scalar", "gpsimd", "sync")


class PB:
    def __init__(self, nc, stack):
        self.nc, self.stack = nc, stack
        self.ops = {e: [] for e in ENGINES}
        self.semh, self.cnt = {}, {}
        self.free = {}

    def sem(self, name):
        self.semh[name] = self.stack.enter_context(self.nc.semaphore(name))
        self.cnt[name] = 0

    def emit(self, eng, fn=None, wait=(), inc=None, inc_n=1):
        post = None
        if inc is not None:
            self.cnt[inc] += inc_n
            post = self.cnt[inc]
        w = [(self.semh[s], v) for (s, v) in wait if v > 0]
        ih = self.semh[inc] if inc is not None else None

        def run(e, w=w, fn=fn, ih=ih, n=inc_n):
            for sh, v in w:
                e.wait_ge(sh, v)
            if fn is not None:
                ins = fn(e)
                if ih is not None:
                    ins.then_inc(ih, n)
            elif ih is not None:
                e.sem_inc(ih, n)

        self.ops[eng].append(run)
        return post

    def dma(self, eng, out, in_, wait=(), inc=None):
        self.cnt[inc] += 16
        post = self.cnt[inc]
        w = [(self.semh[s], v) for (s, v) in wait if v > 0]
        ih = self.semh[inc]

        def run(e, w=w, out=out, in_=in_, ih=ih):
            for sh, v in w:
                e.wait_ge(sh, v)
            e.dma_start(out=out, in_=in_).then_inc(ih, 16)

        self.ops[eng].append(run)
        return post

    def val(self, sem):
        return self.cnt[sem]

    def wfree(self, key):
        return list(self.free.get(key, ()))

    def setfree(self, key, *tokens):
        self.free[key] = [t for t in tokens if t is not None]

    def barrier(self):
        snap = [(s, self.cnt[s]) for s in self.cnt]
        for eng in ENGINES:
            self.emit(eng, fn=None, wait=snap)


def build_program(upto="full"):
    nc = bass.Bass("TRN2", target_bir_lowering=False)
    ap = lambda t: t.ap()

    x0_d = ap(nc.dram_tensor("x0", [NT, 128, D], F32, kind="ExternalInput"))
    w1_d = ap(nc.dram_tensor("w1b", [E, 4, 128, 4, KD, 128], F32, kind="ExternalInput"))
    w2_d = ap(nc.dram_tensor("w2b", [E, 128, KD, FH], F32, kind="ExternalInput"))
    wg_d = ap(nc.dram_tensor("wgb", [128, KD, E], F32, kind="ExternalInput"))
    bg_d = ap(nc.dram_tensor("bgb", [1, E], F32, kind="ExternalInput"))
    wq_d = ap(nc.dram_tensor("wqb", [128, KD, D], F32, kind="ExternalInput"))
    wdkv_d = ap(nc.dram_tensor("wdkvb", [128, KD, L], F32, kind="ExternalInput"))
    wuk_d = ap(nc.dram_tensor("wukb", [128, KL, D], F32, kind="ExternalInput"))
    wuv_d = ap(nc.dram_tensor("wuvb", [128, KL, D], F32, kind="ExternalInput"))
    wo_d = ap(nc.dram_tensor("wob", [128, KD, D], F32, kind="ExternalInput"))
    gam_d = ap(nc.dram_tensor("gamb", [1, L], F32, kind="ExternalInput"))
    bet_d = ap(nc.dram_tensor("betb", [1, L], F32, kind="ExternalInput"))
    invf_d = ap(nc.dram_tensor("invfb", [1, 32], F32, kind="ExternalInput"))

    y_d = ap(nc.dram_tensor("y", [NT, 128, D], F32, kind="ExternalOutput"))
    xs1_d = ap(nc.dram_tensor("xs1", [NT, 128, D], F32, kind="Internal"))
    xs2_d = ap(nc.dram_tensor("xs2", [NT, 128, D], F32, kind="Internal"))
    t1s_d = ap(nc.dram_tensor("t1s", [128, KD, T], BF16, kind="Internal"))
    t2s_d = ap(nc.dram_tensor("t2s", [128, KD, T], BF16, kind="Internal"))
    t3s_d = ap(nc.dram_tensor("t3s", [128, KD, T], BF16, kind="Internal"))
    qscr_d = ap(nc.dram_tensor("qscr", [128, KD, T], BF16, kind="Internal"))
    kscr_d = ap(nc.dram_tensor("kscr", [128, KD, T], BF16, kind="Internal"))
    aos_d = ap(nc.dram_tensor("aos", [128, KD, T], BF16, kind="Internal"))

    from contextlib import ExitStack
    with ExitStack() as stack:
        def SBF(st):
            def f(name, shape, dtype=F32):
                return st.enter_context(nc.sbuf_tensor(name, shape, dtype))
            return f

        sb = SBF(stack)

        def psumt(name, shape, dtype=F32):
            return stack.enter_context(nc.psum_tensor(name, shape, dtype))

        xin = sb("xin", [128, 2, D], F32)
        x1b = sb("x1b", [128, 2, D], F32)
        t2b = sb("t2b", [128, 2, D], BF16)
        latnT = sb("latnT", [128, KL, T], BF16)
        qt = sb("qt", [128, D], F32)
        qkb = sb("qkb", [128, D], BF16)
        rtmp = None
        tstage = sb("tstage", [128, 2, KD, 128], BF16)
        latb = sb("latb", [128, 2, L], F32)
        latn_t = sb("latn_t", [128, L], BF16)
        ceb1 = sb("ceb1", [128, NT, E], F32)
        ceb3 = sb("ceb3", [128, NT, E], F32)
        logit1 = sb("logit1", [128, NT, E], F32)
        logit3 = sb("logit3", [128, NT, E], F32)
        gp = sb("gp", [128, 3, E], F32)
        statb = sb("statb", [128, 16], F32)
        sqb = sb("sqb", [128, D], F32)
        wgs = sb("wgs", [128, KD, E], BF16)
        bgbc = sb("bgbc", [128, E], F32)
        wdkvs = sb("wdkvs", [128, KD, L], BF16)
        wuks = sb("wuks", [128, KL, D], BF16)
        gbb = sb("gbb", [128, 2, L], F32)
        invfs = sb("invfs", [1, 32], F32)
        posr = sb("posr", [1, 128], F32)
        onesr = sb("onesr", [1, 128], F32)
        eyeb = sb("eyeb", [128, 128], BF16)
        eyef = sb("eyef", [128, 128], F32)
        iot = sb("iot", [128, 512], F32)
        angb = sb("angb", [128, 2, 32], F32)
        epsb = sb("epsb", [128, 1], F32)

        PB0 = psumt("PB0", [128, 1024], F32)
        PB1 = psumt("PB1", [128, 1024], F32)
        PB2 = psumt("PB2", [128, 1024], F32)
        PS6 = psumt("PS6", [128, 512], F32)
        PT = psumt("PT", [128, 512], BF16)
        Pp = [PB0[:, 0:512], PB0[:, 512:1024], PB1[:, 0:512], PB1[:, 512:1024],
              PB2[:, 0:512], PB2[:, 512:1024], PS6[:]]

        pb = PB(nc, stack)
        for s in ("pe", "dve", "act", "pool", "dx", "dxs", "dw", "dst"):
            pb.sem(s)
        EMIT, DMA, V, WF, SF = pb.emit, pb.dma, pb.val, pb.wfree, pb.setfree
        ESEM = {"vector": "dve", "scalar": "act", "tensor": "pe", "gpsimd": "pool"}

        def CH(eng, fn, wait=()):
            sm = ESEM[eng]
            return EMIT(eng, fn, wait=list(wait) + [(sm, V(sm))], inc=sm)

        DMA("sync", invfs[:], invf_d, inc="dx")
        DMA("sync", gbb[0:1, 0, :], gam_d, inc="dx")
        DMA("sync", gbb[0:1, 1, :], bet_d, inc="dx")
        DMA("sync", bgbc[0:1, :], bg_d, inc="dx")
        ld_const = V("dx")
        EMIT("gpsimd", lambda e: e.iota(iot[:], pattern=[[1, 512]], channel_multiplier=-1,
                                        allow_small_or_imprecise_dtypes=True), inc="pool")
        g_iota = V("pool")
        EMIT("vector", lambda e: e.memset(onesr[:], 1.0), inc="dve")
        EMIT("vector", lambda e: e.memset(epsb[:], EPS), inc="dve")
        EMIT("vector", lambda e: e.tensor_single_scalar(
            eyeb[:], iot[:, 0:128], 0.0, op=ALU.is_equal),
            wait=[("pool", g_iota)], inc="dve")
        EMIT("vector", lambda e: e.tensor_single_scalar(
            eyef[:], iot[:, 0:128], 0.0, op=ALU.is_equal), inc="dve")
        v_const = V("dve")

        for dst, src, n in ((gbb[:, 0, :], gbb[0:1, 0, :], L),
                            (gbb[:, 1, :], gbb[0:1, 1, :], L),
                            (bgbc[:, :], bgbc[0:1, :], E)):
            EMIT("tensor", lambda e, src=src, n=n: e.matmul(
                Pp[6][:, 0:n], onesr[0:1, :], src, start=True, stop=True),
                wait=[("dx", ld_const), ("dve", v_const)] + WF(("psum", 6)), inc="pe")
            pv = V("pe")
            EMIT("vector", lambda e, dst=dst, n=n: e.tensor_copy(dst, Pp[6][:, 0:n]),
                 wait=[("pe", pv)], inc="dve")
            SF(("psum", 6), ("dve", V("dve")))

        DMA("gpsimd", wgs[:], wg_d, inc="dw")
        DMA("gpsimd", wdkvs[:], wdkv_d, inc="dw")
        DMA("gpsimd", wuks[:], wuk_d, inc="dw")
        ld_small = V("dw")

        def ln_normalize(src_f32, out_bf16, setid, wait0=()):
            sc = 4 * setid
            EMIT("vector", lambda e: e.tensor_reduce(
                statb[:, sc:sc + 1], src_f32, axis=AX.X, op=ALU.add),
                wait=list(wait0) + WF(("lnstat", setid)), inc="dve")
            EMIT("vector", lambda e: e.tensor_tensor(sqb[:], src_f32, src_f32,
                                                     op=ALU.mult), inc="dve")
            CH("vector", lambda e: e.tensor_reduce(
                statb[:, sc + 1:sc + 2], sqb[:], axis=AX.X, op=ALU.add))
            CH("vector", lambda e: e.tensor_scalar_mul(
                statb[:, sc:sc + 1], statb[:, sc:sc + 1], 1.0 / D))
            CH("vector", lambda e: e.tensor_tensor(
                statb[:, sc + 2:sc + 3], statb[:, sc:sc + 1], statb[:, sc:sc + 1],
                op=ALU.mult))
            v_var = CH("vector", lambda e: e.scalar_tensor_tensor(
                statb[:, sc + 2:sc + 3], statb[:, sc + 1:sc + 2], 1.0 / D,
                statb[:, sc + 2:sc + 3], op0=ALU.mult, op1=ALU.subtract))
            a_sig = CH("scalar", lambda e: e.activation(
                statb[:, sc + 2:sc + 3], statb[:, sc + 2:sc + 3], AF.Sqrt, bias=epsb[:]),
                wait=[("dve", v_var)])
            CH("vector", lambda e: e.reciprocal(
                statb[:, sc + 3:sc + 4], statb[:, sc + 2:sc + 3]),
                wait=[("act", a_sig)])
            CH("vector", lambda e: e.tensor_tensor(
                statb[:, sc:sc + 1], statb[:, sc:sc + 1], statb[:, sc + 3:sc + 4],
                op=ALU.mult))
            v_nmr = CH("vector", lambda e: e.tensor_scalar_mul(
                statb[:, sc:sc + 1], statb[:, sc:sc + 1], -1.0))
            a_out = CH("scalar", lambda e: e.activation(
                out_bf16, src_f32, AF.Identity, bias=statb[:, sc:sc + 1],
                scale=statb[:, sc + 3:sc + 4]),
                wait=[("dve", v_nmr)])
            SF(("lnstat", setid), ("act", a_out))
            return a_out

        def transpose8(src_bf16_tile, dst_fn, wait_pe=(), wait_dve=(), kset=None):
            kset = list(range(KD)) if kset is None else list(kset)
            first = True
            for b0 in range(0, len(kset), 4):
                grp = kset[b0:b0 + 4]
                for i, k in enumerate(grp):
                    wl = WF("PT") if i == 0 else []
                    if first and i == 0:
                        wl = wl + list(wait_pe)
                    EMIT("tensor", lambda e, k=k, i=i: e.transpose(
                        PT[:, 128 * i:128 * (i + 1)],
                        src_bf16_tile[:, 128 * k:128 * (k + 1)], eyeb[:]),
                        wait=wl, inc=("pe" if i == len(grp) - 1 else None))
                first = False
                p_tp = V("pe")
                wl = [("pe", p_tp)] + (list(wait_dve) if b0 == 0 else [])
                for i, k in enumerate(grp):
                    EMIT("vector", lambda e, k=k, i=i: e.tensor_copy(
                        dst_fn(k), PT[:, 128 * i:128 * (i + 1)]),
                        wait=wl if i == 0 else (), inc="dve")
                SF("PT", ("dve", V("dve")))
            return V("dve"), V("pe")

        def gating_logits(lhsT_fn, dst_col, pe_wait):
            for k in range(KD):
                wl = []
                if k == 0:
                    wl = [("dw", ld_small)] + list(pe_wait) + WF(("psum", 6))
                EMIT("tensor", lambda e, k=k: e.matmul(
                    Pp[6][:, 0:E], lhsT_fn(k), wgs[:, k, :],
                    start=(k == 0), stop=(k == KD - 1)),
                    wait=wl, inc=("pe" if k == KD - 1 else None))
            pv = V("pe")
            EMIT("vector", lambda e, dst_col=dst_col: e.tensor_copy(dst_col, Pp[6][:, 0:E]),
                 wait=[("pe", pv)], inc="dve")
            SF(("psum", 6), ("dve", V("dve")))
            return V("dve")

        def gating_epilogue(logits, ceb, tile):
            lg = logits[:, tile, :]
            EMIT("vector", lambda e, lg=lg: e.tensor_tensor(lg, lg, bgbc[:, :],
                                                            op=ALU.add),
                 wait=WF("gstat"), inc="dve")
            v_nm = CH("vector", lambda e, lg=lg: e.tensor_reduce(
                statb[:, 8:9], lg, axis=AX.X, op=ALU.max, negate=True))
            a_p = CH("scalar", lambda e, lg=lg: e.activation(
                gp[:, 0, :], lg, AF.Exp, bias=statb[:, 8:9]),
                wait=[("dve", v_nm)])
            CH("vector", lambda e: e.tensor_reduce(
                statb[:, 9:10], gp[:, 0, :], axis=AX.X, op=ALU.max),
                wait=[("act", a_p)])
            CH("vector", lambda e: e.tensor_scalar(
                gp[:, 1, :], gp[:, 0, :], statb[:, 9:10], None, op0=ALU.is_equal))
            CH("vector", lambda e: e.scalar_tensor_tensor(
                gp[:, 1, :], gp[:, 1, :], -2.0, gp[:, 0, :], op0=ALU.mult, op1=ALU.add))
            CH("vector", lambda e: e.tensor_reduce(
                statb[:, 10:11], gp[:, 1, :], axis=AX.X, op=ALU.max))
            CH("vector", lambda e: e.tensor_tensor(
                statb[:, 11:12], statb[:, 9:10], statb[:, 10:11], op=ALU.add))
            CH("vector", lambda e: e.reciprocal(statb[:, 11:12], statb[:, 11:12]))
            CH("vector", lambda e: e.tensor_scalar(
                gp[:, 2, :], gp[:, 0, :], statb[:, 10:11], None, op0=ALU.is_ge))
            CH("vector", lambda e: e.tensor_tensor(
                gp[:, 0, :], gp[:, 0, :], gp[:, 2, :], op=ALU.mult))
            v_last = CH("vector", lambda e, ceb=ceb, tile=tile: e.tensor_scalar(
                ceb[:, tile, :], gp[:, 0, :], statb[:, 11:12], 0.5,
                op0=ALU.mult, op1=ALU.mult))
            SF("gstat", ("dve", v_last), ("act", a_p))
            return v_last

        def lat_ln(a_lat):
            EMIT("vector", lambda e: e.tensor_reduce(
                statb[:, 12:13], latb[:, 0, :], axis=AX.X, op=ALU.add),
                wait=[("act", a_lat)] + WF("latstat"), inc="dve")
            EMIT("vector", lambda e: e.tensor_tensor(
                sqb[:, 0:L], latb[:, 0, :], latb[:, 0, :], op=ALU.mult), inc="dve")
            CH("vector", lambda e: e.tensor_reduce(
                statb[:, 13:14], sqb[:, 0:L], axis=AX.X, op=ALU.add))
            CH("vector", lambda e: e.tensor_scalar_mul(
                statb[:, 12:13], statb[:, 12:13], 1.0 / L))
            CH("vector", lambda e: e.tensor_tensor(
                statb[:, 14:15], statb[:, 12:13], statb[:, 12:13], op=ALU.mult))
            v_var = CH("vector", lambda e: e.scalar_tensor_tensor(
                statb[:, 14:15], statb[:, 13:14], 1.0 / L, statb[:, 14:15],
                op0=ALU.mult, op1=ALU.subtract))
            a_sig = CH("scalar", lambda e: e.activation(
                statb[:, 14:15], statb[:, 14:15], AF.Sqrt, bias=epsb[:]),
                wait=[("dve", v_var)])
            CH("vector", lambda e: e.reciprocal(statb[:, 15:16], statb[:, 14:15]),
                wait=[("act", a_sig)])
            CH("vector", lambda e: e.tensor_tensor(
                statb[:, 12:13], statb[:, 12:13], statb[:, 15:16], op=ALU.mult))
            v_nmr = CH("vector", lambda e: e.tensor_scalar_mul(
                statb[:, 12:13], statb[:, 12:13], -1.0))
            a_z = CH("scalar", lambda e: e.activation(
                latb[:, 1, :], latb[:, 0, :], AF.Identity, bias=statb[:, 12:13],
                scale=statb[:, 15:16]), wait=[("dve", v_nmr)])
            SF("latstat", ("act", a_z))
            CH("vector", lambda e: e.tensor_tensor(
                latb[:, 1, :], latb[:, 1, :], gbb[:, 0, :], op=ALU.mult),
                wait=[("act", a_z)] + WF("latn_t"))
            v_out = CH("vector", lambda e: e.tensor_tensor(
                latn_t[:], latb[:, 1, :], gbb[:, 1, :], op=ALU.add))
            return v_out

        def resid_tile(g, base_load_d, tscr_d, logits, xs_store_d, have_delta,
                       delta_src=None, psum_pair=None):
            s = g % 2
            pp = g % 2
            ld = DMA("sync", xin[:, s, :], base_load_d[g], wait=WF(("xin", s)),
                     inc="dx")
            if have_delta == "buf":
                tt = g % CT
                EMIT("vector", lambda e, s=s, tt=tt, pp=pp: e.tensor_tensor(
                    x1b[:, pp, :], xin[:, s, :], delta_src[:, tt, :], op=ALU.add),
                    wait=[("dx", ld)] + WF(("x1b", pp)), inc="dve")
            elif have_delta == "psum":
                pA, pB, tok = psum_pair
                EMIT("vector", lambda e, s=s, pp=pp, pA=pA: e.tensor_tensor(
                    x1b[:, pp, 0:512], xin[:, s, 0:512], pA[:], op=ALU.add),
                    wait=[("dx", ld)] + WF(("x1b", pp)) + list(tok), inc="dve")
                EMIT("vector", lambda e, s=s, pp=pp, pB=pB: e.tensor_tensor(
                    x1b[:, pp, 512:1024], xin[:, s, 512:1024], pB[:], op=ALU.add),
                    inc="dve")
            v_x = V("dve")
            SF(("xin", s), ("dve", v_x))
            st_x = DMA("sync", xs_store_d[g], x1b[:, pp, :], wait=[("dve", v_x)],
                       inc="dst")
            a_t = ln_normalize(x1b[:, pp, :], t2b[:, pp, :], pp, wait0=WF(("t2b", pp)))
            SF(("x1b", pp), ("act", a_t), ("dst", st_x))
            sg = g % 2
            v_tp, p_tp = transpose8(
                t2b[:, pp, :], lambda k, sg=sg: tstage[:, sg, k, :],
                wait_pe=[("act", a_t)], wait_dve=WF(("tstage", sg)))
            SF(("t2b", pp), ("pe", p_tp))
            st_t = DMA("sync", tscr_d[:, :, 128 * g:128 * (g + 1)], tstage[:, sg, :, :],
                       wait=[("dve", v_tp)], inc="dst")
            if logits is not None:
                pv = gating_logits(lambda k, sg=sg: tstage[:, sg, k, :],
                                   logits[:, g, :], pe_wait=[("dve", v_tp)])
                SF(("tstage", sg), ("dst", st_t), ("pe", V("pe")))
            else:
                SF(("tstage", sg), ("dst", st_t))
            return dict(v_x=v_x, st_x=st_x, st_t=st_t, v_tp=v_tp, sg=sg)

        w1_idx = [0]

        def moe_call(sc, ceb, ce_ready_c, chunk_epilogue, tsrc_d, t_stored_c):
            w1r, w2r, aT, sgbL, delta, t1c = (sc["w1r"], sc["w2r"], sc["aT"],
                                              sc["sgb"], sc["delta"], sc["t1c"])
            t1c_ld = {}

            def ld_chunk(c):
                if c >= NCH:
                    return
                t1c_ld[c] = DMA("scalar", t1c[:, c % 2, :, :],
                                tsrc_d[:, :, CHUNK * c:CHUNK * (c + 1)],
                                wait=WF(("t1c", c % 2)) + [("dst", t_stored_c[c])],
                                inc="dxs")

            groups = [(c, e_i, g) for c in range(NCH) for e_i in range(E)
                      for g in range(4)]
            w1_ld = {}

            def emit_group_dma(gi):
                if gi >= len(groups):
                    return
                c, e_i, g = groups[gi]
                slot = gi % 2
                ld = DMA("gpsimd", w1r[:, slot, :, :, :], w1_d[e_i, g],
                         wait=WF(("w1", slot)), inc="dw")
                w1_ld[gi] = (slot, ld)

            ld_chunk(0)
            ld_chunk(1)
            for gi0 in range(2):
                emit_group_dma(gi0)
            gidx = [0]
            for c in range(NCH):
                tc = t1c[:, c % 2, :, :]
                for e_i in range(E):
                    w2_slot = e_i % 2
                    w2_ld = DMA("gpsimd", w2r[:, w2_slot, :, :], w2_d[e_i],
                                wait=WF(("w2", w2_slot)), inc="dw")
                    for mh in range(8):
                        if mh % 2 == 0:
                            gslot, gld = w1_ld.pop(gidx[0])
                        r = mh % 2
                        tg, tu = 2 * r, 2 * r + 1
                        ld_g = ld_u = gld
                        for nb in range(2):
                            gpar = (mh * 2 + nb) % 2
                            pgp = Pp[gpar]
                            for k in range(KD):
                                wl = []
                                if k == 0:
                                    wl = ([("dw", ld_g), ("dxs", t1c_ld[c])]
                                          + WF(("psum", gpar)))
                                EMIT("tensor", lambda e, k=k, gslot=gslot, tg=tg,
                                     nb=nb, pgp=pgp, tc=tc: e.matmul(
                                         pgp[:], w1r[:, gslot, tg, k, :],
                                         tc[:, k, 512 * nb:512 * (nb + 1)],
                                         start=(k == 0), stop=(k == KD - 1)),
                                     wait=wl, inc=("pe" if k == KD - 1 else None))
                            p_g = V("pe")
                            for k in range(KD):
                                wl = []
                                if k == 0:
                                    wl = [("dw", ld_u)] + WF(("psum", 2))
                                EMIT("tensor", lambda e, k=k, gslot=gslot, tu=tu,
                                     nb=nb, tc=tc: e.matmul(
                                         Pp[2][:], w1r[:, gslot, tu, k, :],
                                         tc[:, k, 512 * nb:512 * (nb + 1)],
                                         start=(k == 0), stop=(k == KD - 1)),
                                     wait=wl, inc=("pe" if k == KD - 1 else None))
                            p_u = V("pe")
                            if nb == 1 and mh % 2 == 1:
                                SF(("w1", gslot), ("pe", p_u))
                            EMIT("scalar", lambda e, pgp=pgp, gpar=gpar: e.activation(
                                sgbL[:, gpar, :], pgp[:], AF.Silu),
                                wait=[("pe", p_g)] + WF(("sgb", gpar)), inc="act")
                            a_sg = V("act")
                            SF(("psum", gpar), ("act", a_sg))
                            EMIT("vector", lambda e, mh=mh, nb=nb, gpar=gpar, aT=aT:
                                 e.tensor_tensor(
                                     aT[:, mh, 512 * nb:512 * (nb + 1)],
                                     Pp[2][:], sgbL[:, gpar, :], op=ALU.mult),
                                 wait=[("pe", p_u), ("act", a_sg)], inc="dve")
                            v_am = V("dve")
                            SF(("psum", 2), ("dve", v_am))
                            SF(("sgb", gpar), ("dve", v_am))
                        if mh % 2 == 1:
                            emit_group_dma(gidx[0] + 2)
                            gidx[0] += 1
                    v_a = V("dve")
                    for tt in range(CT):
                        for nb in range(2):
                            pi2 = 3 + 2 * (tt % 2) + nb
                            pop = Pp[pi2]
                            for k in range(KD):
                                wl = []
                                if k == 0:
                                    wl = ([("dve", v_a), ("dw", w2_ld)]
                                          + WF(("psum", pi2)))
                                EMIT("tensor", lambda e, k=k, tt=tt, nb=nb, pop=pop,
                                     w2_slot=w2_slot, w2r=w2r, aT=aT: e.matmul(
                                         pop[:], aT[:, k, 128 * tt:128 * (tt + 1)],
                                         w2r[:, w2_slot, k, 512 * nb:512 * (nb + 1)],
                                         start=(k == 0), stop=(k == KD - 1)),
                                     wait=wl, inc=("pe" if k == KD - 1 else None))
                            p_o = V("pe")
                            if tt == CT - 1 and nb == 1:
                                SF(("w2", w2_slot), ("pe", p_o))
                            gtile = c * CT + tt
                            if e_i == 0:
                                EMIT("vector", lambda e, tt=tt, nb=nb, pop=pop,
                                     gtile=gtile, ceb=ceb, delta=delta: e.tensor_scalar(
                                         delta[:, tt, 512 * nb:512 * (nb + 1)], pop[:],
                                         ceb[:, gtile, 0:1], None, op0=ALU.mult),
                                     wait=[("pe", p_o),
                                           ("dve", ce_ready_c[c])],
                                     inc="dve")
                            else:
                                EMIT("vector", lambda e, tt=tt, nb=nb, pop=pop,
                                     gtile=gtile, e_i=e_i, ceb=ceb, delta=delta:
                                     e.scalar_tensor_tensor(
                                         delta[:, tt, 512 * nb:512 * (nb + 1)], pop[:],
                                         ceb[:, gtile, e_i:e_i + 1],
                                         delta[:, tt, 512 * nb:512 * (nb + 1)],
                                         op0=ALU.mult, op1=ALU.add),
                                     wait=[("pe", p_o)], inc="dve")
                            SF(("psum", pi2), ("dve", V("dve")))
                SF(("t1c", c % 2), ("pe", V("pe")))
                ld_chunk(c + 2)
                chunk_epilogue(c)

        def rope_tile(g, a_ready, sintok, costok):
            cosb = costok[:, g:g + 1, :].to_broadcast([128, 16, 32])
            sinb = sintok[:, g:g + 1, :].to_broadcast([128, 16, 32])
            x1a = qt[:].rearrange("p (h two d) -> p h two d", h=16, two=2)[:, :, 0, :]
            x2a = qt[:].rearrange("p (h two d) -> p h two d", h=16, two=2)[:, :, 1, :]
            o1 = qkb[:].rearrange("p (h two d) -> p h two d", h=16, two=2)[:, :, 0, :]
            o2 = qkb[:].rearrange("p (h two d) -> p h two d", h=16, two=2)[:, :, 1, :]
            r1 = rtmp[:, 0, :].rearrange("p (h d) -> p h d", h=16)
            r2 = rtmp[:, 1, :].rearrange("p (h d) -> p h d", h=16)
            r3 = rtmp[:, 2, :].rearrange("p (h d) -> p h d", h=16)
            r4 = rtmp[:, 3, :].rearrange("p (h d) -> p h d", h=16)
            EMIT("vector", lambda e: e.tensor_tensor(r1, x1a, cosb, op=ALU.mult),
                 wait=[("act", a_ready)] + WF("qkb"), inc="dve")
            EMIT("vector", lambda e: e.tensor_tensor(r2, x2a, sinb, op=ALU.mult),
                 inc="dve")
            EMIT("vector", lambda e: e.tensor_tensor(r3, x1a, sinb, op=ALU.mult),
                 inc="dve")
            v4 = EMIT("vector", lambda e: e.tensor_tensor(r4, x2a, cosb, op=ALU.mult),
                      inc="dve")
            EMIT("vector", lambda e: e.tensor_tensor(o1, r1, r2, op=ALU.subtract),
                 wait=[("dve", v4 - 2)], inc="dve")
            EMIT("vector", lambda e: e.tensor_tensor(o2, r3, r4, op=ALU.add),
                 wait=[("dve", v4)], inc="dve")
            return V("dve")

        with ExitStack() as scA:
            sba = SBF(scA)
            t1c = sba("t1c", [128, 2, KD, CHUNK], BF16)
            delta = sba("delta", [128, CT, D], F32)
            aT = sba("aT", [128, KD, CHUNK], BF16)
            w1r = sba("w1r", [128, 2, 4, KD, 128], BF16)
            w2r = sba("w2r", [128, 2, KD, FH], BF16)
            sgbA = sba("sgbA", [128, 2, 512], F32)
            scbufs = dict(w1r=w1r, w2r=w2r, aT=aT, sgb=sgbA, delta=delta, t1c=t1c)

            t1_stored_c = {}
            ce1_ready_c = {}
            for c0 in range(NCH):
                for tt in range(CT):
                    g = c0 * CT + tt
                    s = g % 2
                    pp = g % 2
                    ld = DMA("sync", xin[:, s, :], x0_d[g], wait=WF(("xin", s)),
                             inc="dx")
                    a_t1 = ln_normalize(xin[:, s, :], t2b[:, pp, :], pp,
                                        wait0=[("dx", ld)] + WF(("t2b", pp)))
                    SF(("xin", s), ("act", a_t1), ("dve", V("dve")))
                    sg = g % 2
                    v_tp, p_tp = transpose8(
                        t2b[:, pp, :], lambda k, sg=sg: tstage[:, sg, k, :],
                        wait_pe=[("act", a_t1)], wait_dve=WF(("tstage", sg)))
                    SF(("t2b", pp), ("pe", p_tp))
                    st_t = DMA("sync", t1s_d[:, :, 128 * g:128 * (g + 1)],
                               tstage[:, sg, :, :], wait=[("dve", v_tp)], inc="dst")
                    gating_logits(lambda k, sg=sg: tstage[:, sg, k, :],
                                  logit1[:, g, :], pe_wait=[("dve", v_tp)])
                    SF(("tstage", sg), ("dst", st_t), ("pe", V("pe")))
                t1_stored_c[c0] = V("dst")
                for tt in range(CT):
                    gating_epilogue(logit1, ceb1, c0 * CT + tt)
                ce1_ready_c[c0] = V("dve")

            def p1_epilogue(c):
                for tt in range(CT):
                    g = c * CT + tt
                    rt = resid_tile(g, x0_d, t2s_d, None, xs1_d, "buf",
                                    delta_src=delta)
                    sg = rt["sg"]
                    for k in range(KD):
                        wl = (WF(("psum", 2)) + [("dve", rt["v_tp"])]) if k == 0 else []
                        EMIT("tensor", lambda e, k=k, sg=sg: e.matmul(
                            Pp[2][:, 0:L], tstage[:, sg, k, :], wdkvs[:, k, :],
                            start=(k == 0), stop=(k == KD - 1)),
                            wait=wl, inc=("pe" if k == KD - 1 else None))
                    p_lat = V("pe")
                    SF(("tstage", sg), *(WF(("tstage", sg)) + [("pe", p_lat)]))
                    EMIT("scalar", lambda e: e.copy(latb[:, 0, :], Pp[2][:, 0:L]),
                         wait=[("pe", p_lat)] + WF("latb"), inc="act")
                    a_lat = V("act")
                    SF(("psum", 2), ("act", a_lat))
                    v_latn = lat_ln(a_lat)
                    SF("latb", ("dve", v_latn))
                    for kk in range(KL):
                        EMIT("tensor", lambda e, kk=kk: e.transpose(
                            PT[:, 0:128], latn_t[:, 128 * kk:128 * (kk + 1)], eyeb[:]),
                            wait=([("dve", v_latn)] if kk == 0 else []) + WF("PT"),
                            inc="pe")
                        p_tp = V("pe")
                        EMIT("vector", lambda e, kk=kk, g=g: e.tensor_copy(
                            latnT[:, kk, 128 * g:128 * (g + 1)], PT[:, 0:128]),
                            wait=[("pe", p_tp)], inc="dve")
                        SF("PT", ("dve", V("dve")))
                    SF("latn_t", ("pe", V("pe")))

            moe_call(scbufs, ceb1, ce1_ready_c, p1_epilogue, t1s_d, t1_stored_c)
            pb.barrier()
        a_end = {s: V(s) for s in ("pe", "dve", "act", "dst", "dx", "dw")}

        def dump_dram_f32(src_d):
            for g in range(NT):
                s = g % 2
                ld = DMA("sync", xin[:, s, :], src_d[g], wait=WF(("xin", s)), inc="dx")
                EMIT("vector", lambda e, s=s, g=g: e.tensor_copy(
                    x1b[:, g % 2, :], xin[:, s, :]),
                    wait=[("dx", ld)] + WF(("x1b", g % 2)), inc="dve")
                v_y = V("dve")
                SF(("xin", s), ("dve", v_y))
                st = DMA("sync", y_d[g], x1b[:, g % 2, :], wait=[("dve", v_y)],
                         inc="dst")
                SF(("x1b", g % 2), ("dst", st))
            EMIT("sync", fn=None, wait=[("dst", V("dst"))])

        def dump_dram_bf16(src_d):
            for g in range(NT):
                sg = g % 2
                ld = DMA("sync", tstage[:, sg, :, :], src_d[:, :, 128 * g:128 * (g + 1)],
                         wait=WF(("tstage", sg)), inc="dx")
                EMIT("vector", lambda e, sg=sg: e.tensor_copy(
                    x1b[:, sg, :], tstage[:, sg, :, :].rearrange("p k t -> p (k t)")),
                    wait=[("dx", ld)] + WF(("x1b", sg)), inc="dve")
                v_y = V("dve")
                SF(("tstage", sg), ("dve", v_y))
                st = DMA("sync", y_d[g], x1b[:, sg, :], wait=[("dve", v_y)], inc="dst")
                SF(("x1b", sg), ("dst", st))
            EMIT("sync", fn=None, wait=[("dst", V("dst"))])

        if upto not in ("x1", "t1", "t2", "ce"):
            pass
        if upto not in ("x1", "t1", "t2", "ce"):
            with ExitStack() as scB:
                sbb = SBF(scB)
                rtmp = sbb("rtmp", [128, 4, 512], F32)
                t2c = sbb("t2c", [128, KD, CHUNK], BF16)
                wqs = sbb("wqs", [128, KD, D], BF16)
                wuvs = sbb("wuvs", [128, KL, D], BF16)
                wos = sbb("wos", [128, KD, D], BF16)
                sintok = sbb("sintok", [128, NT, 32], F32)
                costok = sbb("costok", [128, NT, 32], F32)
                maskb = sbb("maskb", [128, 4, 512], BF16)
                qhb = sbb("qhb", [64, 2, T], BF16)
                khb = sbb("khb", [64, 2, T], BF16)
                vhb = sbb("vhb", [128, 2, NT, 65], BF16)
                eTb = sbb("eTb", [128, 6, 512], BF16)
                mparts = sbb("mparts", [128, 8], F32)
                mrun = sbb("mrun", [128, 1], F32)
                mhsc = sbb("mhsc", [1, 4], F32)
                qsq = sbb("qsq", [64, T // 2], BF16)
                onesc = sbb("onesc", [64, 1], BF16)
                negm = sbb("negm", [128, 2], F32)
                rcrow = sbb("rcrow", [1, 512], F32)
                rbs = sbb("rbs", [64, 512], F32)
                atmp = sbb("atmp", [64, 512], BF16)
                aoc = sbb("aoc", [128, KD, 512], BF16)

                DMA("gpsimd", wqs[:], wq_d, inc="dw")
                DMA("gpsimd", wuvs[:], wuv_d, inc="dw")
                DMA("gpsimd", wos[:], wo_d, inc="dw")
                ld_bw = V("dw")

                for d in range(4):
                    EMIT("vector", lambda e, d=d: e.tensor_single_scalar(
                        maskb[:, d, :], iot[:], float(128 * d), op=ALU.is_ge),
                        inc="dve")
                for t in range(NT):
                    g_pos = CH("gpsimd", lambda e, t=t: e.iota(
                        posr[:], pattern=[[1, 128]], base=128 * t,
                        channel_multiplier=0, allow_small_or_imprecise_dtypes=True))
                    EMIT("tensor", lambda e: e.matmul(
                        Pp[6][:, 0:32], posr[0:1, :], invfs[0:1, :],
                        start=True, stop=True),
                        wait=[("pool", g_pos)] + WF(("psum", 6)), inc="pe")
                    pv = V("pe")
                    EMIT("vector", lambda e: e.tensor_copy(angb[:, 0, :], Pp[6][:, 0:32]),
                         wait=[("pe", pv)] + WF("angb"), inc="dve")
                    SF(("psum", 6), ("dve", V("dve")))
                    for dst, shift in ((sintok, 0.0), (costok, HALFPI)):
                        if shift:
                            CH("vector", lambda e: e.tensor_scalar_add(
                                angb[:, 0, :], angb[:, 0, :], shift))
                        CH("vector", lambda e: e.tensor_scalar(
                            angb[:, 1, :], angb[:, 0, :], INV2PI, MAGIC,
                            op0=ALU.mult, op1=ALU.add))
                        CH("vector", lambda e: e.tensor_scalar_sub(
                            angb[:, 1, :], angb[:, 1, :], MAGIC))
                        CH("vector", lambda e, dst=dst, t=t: e.scalar_tensor_tensor(
                            dst[:, t, :], angb[:, 1, :], -RC1, angb[:, 0, :],
                            op0=ALU.mult, op1=ALU.add))
                        v_ang = CH("vector", lambda e, dst=dst, t=t: e.scalar_tensor_tensor(
                            dst[:, t, :], angb[:, 1, :], -RC2, dst[:, t, :],
                            op0=ALU.mult, op1=ALU.add))
                    a_s1 = CH("scalar", lambda e, t=t: e.activation(
                        sintok[:, t, :], sintok[:, t, :], AF.Sin),
                        wait=[("dve", v_ang)])
                    a_s2 = CH("scalar", lambda e, t=t: e.activation(
                        costok[:, t, :], costok[:, t, :], AF.Sin))
                    SF("angb", ("act", a_s2))
                tabs_done_a = V("act")

                def qk_proj(g, lhsT_fn, nk, rhs_w, scr_d, extra_wait):
                    for nb in range(2):
                        pq = Pp[nb]
                        for k in range(nk):
                            wl = []
                            if k == 0:
                                wl = (list(extra_wait) + WF(("psum", nb))
                                      + [("dw", ld_bw)])
                            EMIT("tensor", lambda e, k=k, nb=nb, pq=pq: e.matmul(
                                pq[:], lhsT_fn(k),
                                rhs_w[:, k, 512 * nb:512 * (nb + 1)],
                                start=(k == 0), stop=(k == nk - 1)),
                                wait=wl, inc=("pe" if k == nk - 1 else None))
                        p_q = V("pe")
                        EMIT("scalar", lambda e, nb=nb, pq=pq: e.copy(
                            qt[:, 512 * nb:512 * (nb + 1)], pq[:]),
                            wait=[("pe", p_q)] + WF("qt"), inc="act")
                        SF(("psum", nb), ("act", V("act")))
                    a_qt = V("act")
                    v_rope = rope_tile(g, a_qt, sintok, costok)
                    SF("qt", ("dve", v_rope))
                    sg = g % 2
                    v_tp, p_tp = transpose8(
                        qkb[:], lambda k, sg=sg: tstage[:, sg, k, :],
                        wait_pe=[("dve", v_rope), ("act", tabs_done_a)],
                        wait_dve=WF(("tstage", sg)))
                    SF("qkb", ("pe", p_tp))
                    st = DMA("sync", scr_d[:, :, 128 * g:128 * (g + 1)],
                             tstage[:, sg, :, :], wait=[("dve", v_tp)], inc="dst")
                    SF(("tstage", sg), ("dst", st))
                    return st

                qk_stored = 0
                for c in range(NCH):
                    ld_t2 = DMA("scalar", t2c[:, :, :],
                                t2s_d[:, :, CHUNK * c:CHUNK * (c + 1)],
                                wait=WF("t2c"), inc="dxs")
                    for tt in range(CT):
                        g = c * CT + tt
                        qk_proj(g, lambda k, tt=tt: t2c[:, k, 128 * tt:128 * (tt + 1)],
                                KD, wqs, qscr_d, [("dxs", ld_t2)])
                        qk_stored = qk_proj(
                            g, lambda k, g=g: latnT[:, k, 128 * g:128 * (g + 1)],
                            KL, wuks, kscr_d, [])
                    SF("t2c", ("pe", V("pe")))

                PMh = PB2
                PAVs = [PS6, PB2[:, 512:1024]]
                SF(("pA",), *(WF(("psum", 0)) + WF(("psum", 1))))
                SF(("pB",), *(WF(("psum", 2)) + WF(("psum", 3))))
                SF(("pM", 0), *WF(("psum", 4)))
                SF(("pAV", 1), *WF(("psum", 5)))
                SF(("pAV", 0), *WF(("psum", 6)))
                EMIT("vector", lambda e: e.memset(vhb[:, 0, :, 64:65], 1.0), inc="dve")
                EMIT("vector", lambda e: e.memset(vhb[:, 1, :, 64:65], 1.0), inc="dve")
                EMIT("vector", lambda e: e.memset(onesc[:], 1.0), inc="dve")

                ld_q, ld_k = {}, {}

                def head_loads(hn):
                    if hn >= H:
                        return
                    hp = hn % 2
                    kd_h, off = hn // 2, 64 * (hn % 2)
                    ld_q[hn] = DMA("sync", qhb[:, hp, :],
                                   qscr_d[off:off + 64, kd_h, :],
                                   wait=WF(("qh", hp)) +
                                   ([("dst", qk_stored)] if hn < 2 else []), inc="dx")
                    ld_k[hn] = DMA("sync", khb[:, hp, :],
                                   kscr_d[off:off + 64, kd_h, :],
                                   wait=WF(("kh", hp)), inc="dx")

                pm_rr = [0]
                sq_tok = {}

                def emit_item(hn, item):
                    hp = hn % 2
                    kind = item[0]
                    par = 0
                    pmap = PMh[:, 0:512]
                    if kind == "V":
                        tt = item[1]
                        for kl in range(KL):
                            wl = WF(("pM", par)) if kl == 0 else []
                            EMIT("tensor", lambda e, kl=kl, tt=tt, hn=hn, pmap=pmap:
                                 e.matmul(
                                     pmap[:, 0:64],
                                     latnT[:, kl, 128 * tt:128 * (tt + 1)],
                                     wuvs[:, kl, 64 * hn:64 * (hn + 1)],
                                     start=(kl == 0), stop=(kl == KL - 1)),
                                 wait=wl, inc=("pe" if kl == KL - 1 else None))
                        p_v = V("pe")
                        EMIT("scalar", lambda e, tt=tt, hp=hp, pmap=pmap: e.copy(
                            vhb[:, hp, tt, 0:64], pmap[:, 0:64]),
                            wait=[("pe", p_v)] + WF(("vh", hp, tt)), inc="act")
                        SF(("pM", par), ("act", V("act")))
                    elif kind == "SQ":
                        which, hf = item[1], item[2]
                        src = qhb if which == "q" else khb
                        wl = ([("dx", (ld_q if which == "q" else ld_k)[hn])]
                              + WF("qsq"))
                        sq_tok[which] = EMIT(
                            "vector", lambda e, src=src, hp=hp, hf=hf:
                            e.tensor_tensor(
                                qsq[:], src[:, hp, 2048 * hf:2048 * (hf + 1)],
                                src[:, hp, 2048 * hf:2048 * (hf + 1)], op=ALU.mult),
                            wait=wl, inc="dve")
                    elif kind == "NRM":
                        which, j = item[1], item[2]
                        v_sq = sq_tok[which]
                        EMIT("tensor", lambda e, j=j, pmap=pmap: e.matmul(
                            pmap[0:1, 0:512], onesc[:],
                            qsq[:, 512 * (j % 4):512 * (j % 4 + 1)],
                            start=True, stop=True),
                            wait=[("dve", v_sq)] + WF(("pM", par)), inc="pe")
                        p_n = V("pe")
                        col = 0 if which == "q" else 1
                        EMIT("vector", lambda e, pmap=pmap: e.tensor_reduce(
                            mhsc[0:1, 2:3], pmap[0:1, 0:512], axis=AX.X, op=ALU.max),
                            wait=[("pe", p_n)], inc="dve")
                        SF(("pM", par), ("dve", V("dve")))
                        if j == 0:
                            CH("vector", lambda e, col=col: e.tensor_copy(
                                mhsc[0:1, col:col + 1], mhsc[0:1, 2:3]))
                        else:
                            CH("vector", lambda e, col=col: e.tensor_tensor(
                                mhsc[0:1, col:col + 1], mhsc[0:1, col:col + 1],
                                mhsc[0:1, 2:3], op=ALU.max))
                        if j % 4 == 3:
                            SF("qsq", ("pe", p_n))
                    elif kind == "MNB":
                        v_p = CH("vector", lambda e: e.tensor_tensor(
                            mhsc[0:1, 2:3], mhsc[0:1, 0:1], mhsc[0:1, 1:2],
                            op=ALU.mult))
                        a_sq = CH("scalar", lambda e: e.activation(
                            mhsc[0:1, 2:3], mhsc[0:1, 2:3], AF.Sqrt),
                            wait=[("dve", v_p)])
                        v_mh = CH("vector", lambda e: e.tensor_scalar_mul(
                            mhsc[0:1, 3:4], mhsc[0:1, 2:3], -0.125),
                            wait=[("act", a_sq)])
                        EMIT("tensor", lambda e, pmap=pmap: e.matmul(
                            pmap[:, 0:1], onesr[0:1, :], mhsc[0:1, 3:4],
                            start=True, stop=True),
                            wait=[("dve", v_mh)] + WF(("pM", 0)), inc="pe")
                        p_nb = V("pe")
                        EMIT("vector", lambda e, hp=hp, pmap=pmap: e.tensor_copy(
                            negm[:, hp:hp + 1], pmap[:, 0:1]),
                            wait=[("pe", p_nb)], inc="dve")
                        SF(("pM", par), ("dve", V("dve")))
                        SF(("negm", hp), ("dve", V("dve")))

                def make_items(hn):
                    items = []
                    if hn >= H:
                        return items
                    for tt in range(NT):
                        items.append(("V", tt))
                    for which in ("q", "k"):
                        for hf in range(2):
                            items.append(("SQ", which, hf))
                            for j in range(4 * hf, 4 * hf + 4):
                                items.append(("NRM", which, j))
                    items.append(("MNB",))
                    return items

                head_loads(0)
                head_loads(1)
                for it in make_items(0):
                    emit_item(0, it)
                for h in range(H):
                    hp = h % 2
                    kd_h, off = h // 2, 64 * (h % 2)
                    items = make_items(h + 1)
                    qi = [0]

                    def drip(n=3):
                        for _ in range(n):
                            if qi[0] < len(items):
                                emit_item(h + 1, items[qi[0]])
                                qi[0] += 1

                    v_negm = [t for t in WF(("negm", hp))]
                    pend_epi = [None]

                    def emit_epi():
                        if pend_epi[0] is None:
                            return
                        PAVe, avkeye, st_e = pend_epi[0]
                        pend_epi[0] = None
                        p_av = V("pe")
                        EMIT("vector", lambda e, PAVe=PAVe: e.tensor_scalar_max(
                            rcrow[:], PAVe[64:65, :], 1.0e-30),
                            wait=WF(("epi_pav", st_e % 2)), inc="dve")
                        v_rc = CH("vector", lambda e: e.reciprocal(
                            rcrow[:], rcrow[:]))
                        pmap3 = PMh[:, 0:512]
                        EMIT("tensor", lambda e, pmap3=pmap3: e.matmul(
                            pmap3[0:64, :], onesr[0:1, 0:64], rcrow[0:1, :],
                            start=True, stop=True),
                            wait=[("dve", v_rc)] + WF(("pM", 0)), inc="pe")
                        p_rb = V("pe")
                        EMIT("vector", lambda e, pmap3=pmap3: e.tensor_copy(
                            rbs[:], pmap3[0:64, :]),
                            wait=[("pe", p_rb)] + WF("rbs"), inc="dve")
                        SF(("pM", 0), ("dve", V("dve")))
                        v_at = CH("vector", lambda e, PAVe=PAVe: e.tensor_tensor(
                            atmp[:], PAVe[0:64, :], rbs[:], op=ALU.mult),
                            wait=WF("atmp"))
                        SF(avkeye, ("dve", v_at))
                        SF("rbs", ("dve", v_at))
                        st_ao = DMA("sync",
                                    aos_d[off:off + 64, kd_h,
                                          512 * st_e:512 * (st_e + 1)],
                                    atmp[:], wait=[("dve", v_at)], inc="dst")
                        SF("atmp", ("dst", st_ao))

                    for st_i in range(8):
                        jmax = 4 * (st_i + 1)
                        npairs = jmax // 2
                        PAV = PAVs[st_i % 2]
                        avkey = ("pAV", st_i % 2)
                        pend = None
                        for jp in range(npairs + 1):
                            if jp < npairs:
                                j0 = 2 * jp
                                ppar = jp % 2
                                PAx = PB0 if ppar == 0 else PB1
                                pkey = ("pA",) if ppar == 0 else ("pB",)
                                epar = jp % 3
                                for jj in (j0, j0 + 1):
                                    wl = []
                                    if jj == j0:
                                        wl = WF(pkey)
                                    EMIT("tensor", lambda e, jj=jj, st_i=st_i, hp=hp,
                                         PAx=PAx: e.matmul(
                                             PAx[:, 512 * (jj % 2):512 * (jj % 2 + 1)],
                                             khb[:, hp, 128 * jj:128 * (jj + 1)],
                                             qhb[:, hp, 512 * st_i:512 * (st_i + 1)],
                                             start=True, stop=True),
                                         wait=wl, inc=("pe" if jj > j0 else None))
                                p_s = V("pe")
                                ew = ([("pe", p_s)] + WF(("eTp", epar))
                                      + list(v_negm))
                                EMIT("scalar", lambda e, epar=epar, hp=hp, PAx=PAx:
                                     e.activation(
                                         eTb[:, 2 * epar:2 * epar + 2, :].rearrange(
                                             "p a t -> p (a t)"),
                                         PAx[:], AF.Exp,
                                         bias=negm[:, hp:hp + 1], scale=0.125),
                                     wait=ew, inc="act")
                                a_e = V("act")
                                SF(pkey, ("act", a_e))
                                toks = {}
                                for jj in (j0, j0 + 1):
                                    di = jj - 4 * st_i
                                    esl = 2 * epar + (jj - j0)
                                    if di >= 0:
                                        EMIT("vector", lambda e, esl=esl, di=di:
                                             e.tensor_tensor(
                                                 eTb[:, esl, :], eTb[:, esl, :],
                                                 maskb[:, di, :], op=ALU.mult),
                                             wait=[("act", a_e)], inc="dve")
                                        toks[jj] = ("dve", V("dve"))
                                    else:
                                        toks[jj] = ("act", a_e)
                                drip(1)
                                if jp == 1:
                                    emit_epi()
                            if pend is not None:
                                pj0, pepar, ptoks = pend
                                for jj in (pj0, pj0 + 1):
                                    esl = 2 * pepar + (jj - pj0)
                                    wl = [ptoks[jj]]
                                    if jj == 0:
                                        wl += list(avkey and WF(avkey))
                                    EMIT("tensor", lambda e, jj=jj, hp=hp, esl=esl,
                                         jmax=jmax, PAV=PAV: e.matmul(
                                             PAV[0:65, :], vhb[:, hp, jj, 0:65],
                                             eTb[:, esl, :],
                                             start=(jj == 0), stop=(jj == jmax - 1)),
                                         wait=wl, inc="pe")
                                SF(("eTp", pepar), ("pe", V("pe")))
                                drip(1)
                            if jp < npairs:
                                pend = (j0, epar, toks)
                        SF(("epi_pav", st_i % 2), ("pe", V("pe")))
                        pend_epi[0] = (PAV, avkey, st_i)
                    emit_epi()
                    while qi[0] < len(items):
                        emit_item(h + 1, items[qi[0]])
                        qi[0] += 1
                    p_heads_end = V("pe")
                    SF(("qh", hp), ("pe", p_heads_end))
                    SF(("kh", hp), ("pe", p_heads_end))
                    for tt in range(NT):
                        SF(("vh", hp, tt), ("pe", p_heads_end))
                    head_loads(h + 2)
                SF(("psum", 0), *WF(("pA",)))
                SF(("psum", 1), *WF(("pA",)))
                SF(("psum", 2), *WF(("pB",)))
                SF(("psum", 3), *WF(("pB",)))
                SF(("psum", 4), *WF(("pM", 0)))
                SF(("psum", 5), *WF(("pAV", 1)))
                SF(("psum", 6), *WF(("pAV", 0)))
                aos_stored = V("dst")

                for st_i in range(8):
                    ld_ao = DMA("scalar", aoc[:, :, :],
                                aos_d[:, :, 512 * st_i:512 * (st_i + 1)],
                                wait=WF("aoc") +
                                ([("dst", aos_stored)] if st_i == 0 else []),
                                inc="dxs")
                    for tl in range(4):
                        g = 4 * st_i + tl
                        p_o = None
                        for nb in range(2):
                            pio = 2 * (tl % 2) + nb
                            pop = Pp[pio]
                            for k in range(KD):
                                wl = []
                                if k == 0:
                                    wl = [("dxs", ld_ao)] + WF(("psum", pio))
                                EMIT("tensor", lambda e, k=k, tl=tl, nb=nb, pop=pop:
                                     e.matmul(
                                         pop[:], aoc[:, k, 128 * tl:128 * (tl + 1)],
                                         wos[:, k, 512 * nb:512 * (nb + 1)],
                                         start=(k == 0), stop=(k == KD - 1)),
                                     wait=wl, inc=("pe" if k == KD - 1 else None))
                            p_o = V("pe")
                        pA, pB = Pp[2 * (tl % 2)], Pp[2 * (tl % 2) + 1]
                        rt = resid_tile(g, xs1_d, t3s_d, logit3, xs2_d, "psum",
                                        psum_pair=(pA, pB, [("pe", p_o)]))
                        SF(("psum", 2 * (tl % 2)), ("dve", rt["v_x"]))
                        SF(("psum", 2 * (tl % 2) + 1), ("dve", rt["v_x"]))
                    SF("aoc", ("pe", V("pe")))
                t3_stored = V("dst")
                for g in range(NT):
                    gating_epilogue(logit3, ceb3, g)
                v_ce3 = V("dve")
                t3_stored_c = {c: t3_stored for c in range(NCH)}
                ce3_ready_c = {c: v_ce3 for c in range(NCH)}
                pb.barrier()

            if upto == "x2":
                dump_dram_f32(xs2_d)
            elif upto == "ao":
                dump_dram_bf16(aos_d)
            else:
                with ExitStack() as scC:
                    sbc = SBF(scC)
                    t3c = sbc("t3c", [128, 2, KD, CHUNK], BF16)
                    deltaC = sbc("deltaC", [128, CT, D], F32)
                    aTC = sbc("aTC", [128, KD, CHUNK], BF16)
                    w1rC = sbc("w1rC", [128, 2, 4, KD, 128], BF16)
                    w2rC = sbc("w2rC", [128, 2, KD, FH], BF16)
                    sgbC = sbc("sgbC", [128, 2, 512], F32)
                    scbufsC = dict(w1r=w1rC, w2r=w2rC, aT=aTC, sgb=sgbC,
                                   delta=deltaC, t1c=t3c)

                    def p3_epilogue(c):
                        for tt in range(CT):
                            g = c * CT + tt
                            sx = g % 2
                            ld = DMA("sync", xin[:, sx, :], xs2_d[g],
                                     wait=WF(("xin", sx)), inc="dx")
                            EMIT("vector", lambda e, sx=sx, tt=tt: e.tensor_tensor(
                                x1b[:, sx, :], xin[:, sx, :], deltaC[:, tt, :],
                                op=ALU.add),
                                wait=[("dx", ld)] + WF(("x1b", sx)), inc="dve")
                            v_y = V("dve")
                            SF(("xin", sx), ("dve", v_y))
                            st = DMA("sync", y_d[g], x1b[:, sx, :],
                                     wait=[("dve", v_y)], inc="dst")
                            SF(("x1b", sx), ("dst", st))

                    moe_call(scbufsC, ceb3, ce3_ready_c, p3_epilogue, t3s_d, t3_stored_c)
                    EMIT("sync", fn=None, wait=[("dst", V("dst"))])

        if upto == "x1":
            dump_dram_f32(xs1_d)
        elif upto == "t1":
            dump_dram_bf16(t1s_d)
        elif upto == "t2":
            dump_dram_bf16(t2s_d)
        elif upto == "ao":
            dump_dram_bf16(aos_d)
        elif upto == "ce":
            for g in range(NT):
                EMIT("vector", lambda e, g=g: e.tensor_copy(
                    x1b[:, g % 2, 0:E], ceb1[:, g, :]),
                    wait=WF(("x1b", g % 2)), inc="dve")
                EMIT("vector", lambda e, g=g: e.memset(x1b[:, g % 2, E:D], 0.0),
                     inc="dve")
                v_y = V("dve")
                st = DMA("sync", y_d[g], x1b[:, g % 2, :], wait=[("dve", v_y)],
                         inc="dst")
                SF(("x1b", g % 2), ("dst", st))
            EMIT("sync", fn=None, wait=[("dst", V("dst"))])

        with nc.Block() as block:
            for eng in ENGINES:
                def make(eng):
                    def body(e):
                        for f in pb.ops[eng]:
                            f(e)
                    return body
                getattr(block, eng)(make(eng))

    return nc


def _prep_weights(i):
    f32 = np.float32

    def pmaj(w, kt, x):
        return np.ascontiguousarray(
            np.asarray(w, dtype=f32).reshape(kt, 128, x).transpose(1, 0, 2))

    w1 = np.ascontiguousarray(i["L0_ffn_w1"], dtype=f32)
    w1b = np.ascontiguousarray(
        w1.reshape(E, 2, 4, 2, 128, KD, 128).transpose(0, 2, 6, 3, 1, 5, 4)
        .reshape(E, 4, 128, 4, KD, 128))
    w2 = np.ascontiguousarray(i["L0_ffn_w2"], dtype=f32)
    w2b = np.ascontiguousarray(
        w2.transpose(0, 2, 1).reshape(E, KD, 128, FH).transpose(0, 2, 1, 3))
    wgb = pmaj(np.asarray(i["L0_ffn_Wg"], dtype=f32).T, KD, E)
    bgb = np.ascontiguousarray(np.asarray(i["L0_ffn_bg"], dtype=f32).reshape(1, E))
    wqb = pmaj(i["L0_attn_W_q"], KD, D)
    wdkvb = pmaj(i["L0_attn_W_dkv"], KD, L)
    wukb = pmaj(i["L0_attn_W_uk"], KL, D)
    wuvb = pmaj(i["L0_attn_W_uv"], KL, D)
    wob = pmaj(i["L0_attn_W_o"], KD, D)
    gamb = np.ascontiguousarray(np.asarray(i["L0_attn_ln_gamma"], dtype=f32).reshape(1, L))
    betb = np.ascontiguousarray(np.asarray(i["L0_attn_ln_beta"], dtype=f32).reshape(1, L))
    invfb = (1.0 / (10000.0 ** (np.arange(32, dtype=np.float64) / 32))).astype(
        f32).reshape(1, 32)
    return dict(w1b=w1b, w2b=w2b, wgb=wgb, bgb=bgb, wqb=wqb, wdkvb=wdkvb,
                wukb=wukb, wuvb=wuvb, wob=wob, gamb=gamb, betb=betb, invfb=invfb)


def kernel(_upto="full", _cores=8, **inputs):
    x = np.ascontiguousarray(np.asarray(inputs["L0_attn_x"]), dtype=np.float32)
    wb = _prep_weights({k: np.asarray(v) for k, v in inputs.items()})
    nc = build_program(upto=_upto)
    in_maps = []
    for c in range(_cores):
        m = dict(wb)
        m["x0"] = np.ascontiguousarray(x[c].reshape(NT, 128, D))
        in_maps.append(m)
    res = run_bass_kernel_spmd(nc, in_maps, core_ids=list(range(_cores)))
    return np.stack([r["y"].reshape(T, D) for r in res.results])


# version 10 (final)
# speedup vs baseline: 1.0672x; 1.0254x over previous
import numpy as np
import ml_dtypes
import concourse.bass as bass
import concourse.mybir as mybir
from concourse.bass_utils import run_bass_kernel_spmd

F32, BF16 = mybir.dt.float32, mybir.dt.bfloat16
AF = mybir.ActivationFunctionType
ALU = mybir.AluOpType
AX = mybir.AxisListType

T, D, H, DH, L, E, FH = 4096, 1024, 16, 64, 256, 8, 1024
NT = T // 128
CHUNK = 1024
NCH = T // CHUNK
CT = CHUNK // 128
KD = D // 128
KL = L // 128
EPS = 1e-5
INV2PI = float(np.float32(1.0 / (2 * np.pi)))
MAGIC = float(np.float32(3 * 2.0 ** 22))
RC1 = float(np.float32(6.28125))
RC2 = float(np.float32(2 * np.pi - 6.28125))
HALFPI = float(np.float32(np.pi / 2))

ENGINES = ("tensor", "vector", "scalar", "gpsimd", "sync")


class PB:
    def __init__(self, nc, stack):
        self.nc, self.stack = nc, stack
        self.ops = {e: [] for e in ENGINES}
        self.semh, self.cnt = {}, {}
        self.free = {}

    def sem(self, name):
        self.semh[name] = self.stack.enter_context(self.nc.semaphore(name))
        self.cnt[name] = 0

    def emit(self, eng, fn=None, wait=(), inc=None, inc_n=1):
        post = None
        if inc is not None:
            self.cnt[inc] += inc_n
            post = self.cnt[inc]
        w = [(self.semh[s], v) for (s, v) in wait if v > 0]
        ih = self.semh[inc] if inc is not None else None

        def run(e, w=w, fn=fn, ih=ih, n=inc_n):
            for sh, v in w:
                e.wait_ge(sh, v)
            if fn is not None:
                ins = fn(e)
                if ih is not None:
                    ins.then_inc(ih, n)
            elif ih is not None:
                e.sem_inc(ih, n)

        self.ops[eng].append(run)
        return post

    def dma(self, eng, out, in_, wait=(), inc=None):
        self.cnt[inc] += 16
        post = self.cnt[inc]
        w = [(self.semh[s], v) for (s, v) in wait if v > 0]
        ih = self.semh[inc]

        def run(e, w=w, out=out, in_=in_, ih=ih):
            for sh, v in w:
                e.wait_ge(sh, v)
            e.dma_start(out=out, in_=in_).then_inc(ih, 16)

        self.ops[eng].append(run)
        return post

    def val(self, sem):
        return self.cnt[sem]

    def wfree(self, key):
        return list(self.free.get(key, ()))

    def setfree(self, key, *tokens):
        self.free[key] = [t for t in tokens if t is not None]

    def barrier(self):
        snap = [(s, self.cnt[s]) for s in self.cnt]
        for eng in ENGINES:
            self.emit(eng, fn=None, wait=snap)


def build_program(upto="full"):
    nc = bass.Bass("TRN2", target_bir_lowering=False)
    ap = lambda t: t.ap()

    x0_d = ap(nc.dram_tensor("x0", [NT, 128, D], F32, kind="ExternalInput"))
    w1_d = ap(nc.dram_tensor("w1b", [E, 4, 128, 4, KD, 128], F32, kind="ExternalInput"))
    w2_d = ap(nc.dram_tensor("w2b", [E, 128, KD, FH], F32, kind="ExternalInput"))
    wg_d = ap(nc.dram_tensor("wgb", [128, KD, E], F32, kind="ExternalInput"))
    bg_d = ap(nc.dram_tensor("bgb", [1, E], F32, kind="ExternalInput"))
    wq_d = ap(nc.dram_tensor("wqb", [128, KD, D], F32, kind="ExternalInput"))
    wdkv_d = ap(nc.dram_tensor("wdkvb", [128, KD, L], F32, kind="ExternalInput"))
    wuk_d = ap(nc.dram_tensor("wukb", [128, KL, D], F32, kind="ExternalInput"))
    wuv_d = ap(nc.dram_tensor("wuvb", [128, KL, D], F32, kind="ExternalInput"))
    wo_d = ap(nc.dram_tensor("wob", [128, KD, D], F32, kind="ExternalInput"))
    gam_d = ap(nc.dram_tensor("gamb", [1, L], F32, kind="ExternalInput"))
    bet_d = ap(nc.dram_tensor("betb", [1, L], F32, kind="ExternalInput"))
    invf_d = ap(nc.dram_tensor("invfb", [1, 32], F32, kind="ExternalInput"))

    y_d = ap(nc.dram_tensor("y", [NT, 128, D], F32, kind="ExternalOutput"))
    xs1_d = ap(nc.dram_tensor("xs1", [NT, 128, D], F32, kind="Internal"))
    xs2_d = ap(nc.dram_tensor("xs2", [NT, 128, D], F32, kind="Internal"))
    t1s_d = ap(nc.dram_tensor("t1s", [128, KD, T], BF16, kind="Internal"))
    t2s_d = ap(nc.dram_tensor("t2s", [128, KD, T], BF16, kind="Internal"))
    t3s_d = ap(nc.dram_tensor("t3s", [128, KD, T], BF16, kind="Internal"))
    qscr_d = ap(nc.dram_tensor("qscr", [128, KD, T], BF16, kind="Internal"))
    kscr_d = ap(nc.dram_tensor("kscr", [128, KD, T], BF16, kind="Internal"))
    aos_d = ap(nc.dram_tensor("aos", [128, KD, T], BF16, kind="Internal"))

    from contextlib import ExitStack
    with ExitStack() as stack:
        def SBF(st):
            def f(name, shape, dtype=F32):
                return st.enter_context(nc.sbuf_tensor(name, shape, dtype))
            return f

        sb = SBF(stack)

        def psumt(name, shape, dtype=F32):
            return stack.enter_context(nc.psum_tensor(name, shape, dtype))

        xin = sb("xin", [128, 2, D], F32)
        x1b = sb("x1b", [128, 2, D], F32)
        t2b = sb("t2b", [128, 2, D], BF16)
        latnT = sb("latnT", [128, KL, T], BF16)
        qt = qkb = None
        rtmp = None
        tstage = sb("tstage", [128, 2, KD, 128], BF16)
        latb = sb("latb", [128, 2, L], F32)
        latn_t = sb("latn_t", [128, L], BF16)
        ceb1 = sb("ceb1", [128, NT, E], F32)
        ceb3 = sb("ceb3", [128, NT, E], F32)
        logit1 = sb("logit1", [128, NT, E], F32)
        logit3 = sb("logit3", [128, NT, E], F32)
        gp = sb("gp", [128, 3, E], F32)
        statb = sb("statb", [128, 16], F32)
        sqb = sb("sqb", [128, D], F32)
        wgs = sb("wgs", [128, KD, E], BF16)
        bgbc = sb("bgbc", [128, E], F32)
        wdkvs = sb("wdkvs", [128, KD, L], BF16)
        wuks = sb("wuks", [128, KL, D], BF16)
        gbb = sb("gbb", [128, 2, L], F32)
        invfs = sb("invfs", [1, 32], F32)
        posr = sb("posr", [1, 128], F32)
        onesr = sb("onesr", [1, 128], F32)
        eyeb = sb("eyeb", [128, 128], BF16)
        eyef = sb("eyef", [128, 128], F32)
        iot = sb("iot", [128, 512], F32)
        angb = sb("angb", [128, 2, 32], F32)
        epsb = sb("epsb", [128, 1], F32)

        PB0 = psumt("PB0", [128, 1024], F32)
        PB1 = psumt("PB1", [128, 1024], F32)
        PB2 = psumt("PB2", [128, 1024], F32)
        PS6 = psumt("PS6", [128, 512], F32)
        PT = psumt("PT", [128, 512], BF16)
        Pp = [PB0[:, 0:512], PB0[:, 512:1024], PB1[:, 0:512], PB1[:, 512:1024],
              PB2[:, 0:512], PB2[:, 512:1024], PS6[:]]

        pb = PB(nc, stack)
        for s in ("pe", "dve", "act", "pool", "dx", "dxs", "dw", "dst"):
            pb.sem(s)
        EMIT, DMA, V, WF, SF = pb.emit, pb.dma, pb.val, pb.wfree, pb.setfree
        ESEM = {"vector": "dve", "scalar": "act", "tensor": "pe", "gpsimd": "pool"}

        def CH(eng, fn, wait=()):
            sm = ESEM[eng]
            return EMIT(eng, fn, wait=list(wait) + [(sm, V(sm))], inc=sm)

        DMA("sync", invfs[:], invf_d, inc="dx")
        DMA("sync", gbb[0:1, 0, :], gam_d, inc="dx")
        DMA("sync", gbb[0:1, 1, :], bet_d, inc="dx")
        DMA("sync", bgbc[0:1, :], bg_d, inc="dx")
        ld_const = V("dx")
        EMIT("gpsimd", lambda e: e.iota(iot[:], pattern=[[1, 512]], channel_multiplier=-1,
                                        allow_small_or_imprecise_dtypes=True), inc="pool")
        g_iota = V("pool")
        EMIT("vector", lambda e: e.memset(onesr[:], 1.0), inc="dve")
        EMIT("vector", lambda e: e.memset(epsb[:], EPS), inc="dve")
        EMIT("vector", lambda e: e.tensor_single_scalar(
            eyeb[:], iot[:, 0:128], 0.0, op=ALU.is_equal),
            wait=[("pool", g_iota)], inc="dve")
        EMIT("vector", lambda e: e.tensor_single_scalar(
            eyef[:], iot[:, 0:128], 0.0, op=ALU.is_equal), inc="dve")
        v_const = V("dve")

        for dst, src, n in ((gbb[:, 0, :], gbb[0:1, 0, :], L),
                            (gbb[:, 1, :], gbb[0:1, 1, :], L),
                            (bgbc[:, :], bgbc[0:1, :], E)):
            EMIT("tensor", lambda e, src=src, n=n: e.matmul(
                Pp[6][:, 0:n], onesr[0:1, :], src, start=True, stop=True),
                wait=[("dx", ld_const), ("dve", v_const)] + WF(("psum", 6)), inc="pe")
            pv = V("pe")
            EMIT("vector", lambda e, dst=dst, n=n: e.tensor_copy(dst, Pp[6][:, 0:n]),
                 wait=[("pe", pv)], inc="dve")
            SF(("psum", 6), ("dve", V("dve")))

        DMA("gpsimd", wgs[:], wg_d, inc="dw")
        DMA("gpsimd", wdkvs[:], wdkv_d, inc="dw")
        DMA("gpsimd", wuks[:], wuk_d, inc="dw")
        ld_small = V("dw")

        def ln_normalize(src_f32, out_bf16, setid, wait0=()):
            sc = 4 * setid
            EMIT("vector", lambda e: e.tensor_reduce(
                statb[:, sc:sc + 1], src_f32, axis=AX.X, op=ALU.add),
                wait=list(wait0) + WF(("lnstat", setid)), inc="dve")
            EMIT("vector", lambda e: e.tensor_tensor(sqb[:], src_f32, src_f32,
                                                     op=ALU.mult), inc="dve")
            CH("vector", lambda e: e.tensor_reduce(
                statb[:, sc + 1:sc + 2], sqb[:], axis=AX.X, op=ALU.add))
            CH("vector", lambda e: e.tensor_scalar_mul(
                statb[:, sc:sc + 1], statb[:, sc:sc + 1], 1.0 / D))
            CH("vector", lambda e: e.tensor_tensor(
                statb[:, sc + 2:sc + 3], statb[:, sc:sc + 1], statb[:, sc:sc + 1],
                op=ALU.mult))
            v_var = CH("vector", lambda e: e.scalar_tensor_tensor(
                statb[:, sc + 2:sc + 3], statb[:, sc + 1:sc + 2], 1.0 / D,
                statb[:, sc + 2:sc + 3], op0=ALU.mult, op1=ALU.subtract))
            a_sig = CH("scalar", lambda e: e.activation(
                statb[:, sc + 2:sc + 3], statb[:, sc + 2:sc + 3], AF.Sqrt, bias=epsb[:]),
                wait=[("dve", v_var)])
            CH("vector", lambda e: e.reciprocal(
                statb[:, sc + 3:sc + 4], statb[:, sc + 2:sc + 3]),
                wait=[("act", a_sig)])
            CH("vector", lambda e: e.tensor_tensor(
                statb[:, sc:sc + 1], statb[:, sc:sc + 1], statb[:, sc + 3:sc + 4],
                op=ALU.mult))
            v_nmr = CH("vector", lambda e: e.tensor_scalar_mul(
                statb[:, sc:sc + 1], statb[:, sc:sc + 1], -1.0))
            a_out = CH("scalar", lambda e: e.activation(
                out_bf16, src_f32, AF.Identity, bias=statb[:, sc:sc + 1],
                scale=statb[:, sc + 3:sc + 4]),
                wait=[("dve", v_nmr)])
            SF(("lnstat", setid), ("act", a_out))
            return a_out

        def transpose8(src_bf16_tile, dst_fn, wait_pe=(), wait_dve=(), kset=None):
            kset = list(range(KD)) if kset is None else list(kset)
            first = True
            for b0 in range(0, len(kset), 4):
                grp = kset[b0:b0 + 4]
                for i, k in enumerate(grp):
                    wl = WF("PT") if i == 0 else []
                    if first and i == 0:
                        wl = wl + list(wait_pe)
                    EMIT("tensor", lambda e, k=k, i=i: e.transpose(
                        PT[:, 128 * i:128 * (i + 1)],
                        src_bf16_tile[:, 128 * k:128 * (k + 1)], eyeb[:]),
                        wait=wl, inc=("pe" if i == len(grp) - 1 else None))
                first = False
                p_tp = V("pe")
                wl = [("pe", p_tp)] + (list(wait_dve) if b0 == 0 else [])
                for i, k in enumerate(grp):
                    EMIT("vector", lambda e, k=k, i=i: e.tensor_copy(
                        dst_fn(k), PT[:, 128 * i:128 * (i + 1)]),
                        wait=wl if i == 0 else (), inc="dve")
                SF("PT", ("dve", V("dve")))
            return V("dve"), V("pe")

        def gating_logits(lhsT_fn, dst_col, pe_wait):
            for k in range(KD):
                wl = []
                if k == 0:
                    wl = [("dw", ld_small)] + list(pe_wait) + WF(("psum", 6))
                EMIT("tensor", lambda e, k=k: e.matmul(
                    Pp[6][:, 0:E], lhsT_fn(k), wgs[:, k, :],
                    start=(k == 0), stop=(k == KD - 1)),
                    wait=wl, inc=("pe" if k == KD - 1 else None))
            pv = V("pe")
            EMIT("vector", lambda e, dst_col=dst_col: e.tensor_copy(dst_col, Pp[6][:, 0:E]),
                 wait=[("pe", pv)], inc="dve")
            SF(("psum", 6), ("dve", V("dve")))
            return V("dve")

        def gating_epilogue(logits, ceb, tile):
            lg = logits[:, tile, :]
            EMIT("vector", lambda e, lg=lg: e.tensor_tensor(lg, lg, bgbc[:, :],
                                                            op=ALU.add),
                 wait=WF("gstat"), inc="dve")
            v_nm = CH("vector", lambda e, lg=lg: e.tensor_reduce(
                statb[:, 8:9], lg, axis=AX.X, op=ALU.max, negate=True))
            a_p = CH("scalar", lambda e, lg=lg: e.activation(
                gp[:, 0, :], lg, AF.Exp, bias=statb[:, 8:9]),
                wait=[("dve", v_nm)])
            CH("vector", lambda e: e.tensor_reduce(
                statb[:, 9:10], gp[:, 0, :], axis=AX.X, op=ALU.max),
                wait=[("act", a_p)])
            CH("vector", lambda e: e.tensor_scalar(
                gp[:, 1, :], gp[:, 0, :], statb[:, 9:10], None, op0=ALU.is_equal))
            CH("vector", lambda e: e.scalar_tensor_tensor(
                gp[:, 1, :], gp[:, 1, :], -2.0, gp[:, 0, :], op0=ALU.mult, op1=ALU.add))
            CH("vector", lambda e: e.tensor_reduce(
                statb[:, 10:11], gp[:, 1, :], axis=AX.X, op=ALU.max))
            CH("vector", lambda e: e.tensor_tensor(
                statb[:, 11:12], statb[:, 9:10], statb[:, 10:11], op=ALU.add))
            CH("vector", lambda e: e.reciprocal(statb[:, 11:12], statb[:, 11:12]))
            CH("vector", lambda e: e.tensor_scalar(
                gp[:, 2, :], gp[:, 0, :], statb[:, 10:11], None, op0=ALU.is_ge))
            CH("vector", lambda e: e.tensor_tensor(
                gp[:, 0, :], gp[:, 0, :], gp[:, 2, :], op=ALU.mult))
            v_last = CH("vector", lambda e, ceb=ceb, tile=tile: e.tensor_scalar(
                ceb[:, tile, :], gp[:, 0, :], statb[:, 11:12], 0.5,
                op0=ALU.mult, op1=ALU.mult))
            SF("gstat", ("dve", v_last), ("act", a_p))
            return v_last

        def lat_ln(a_lat):
            EMIT("vector", lambda e: e.tensor_reduce(
                statb[:, 12:13], latb[:, 0, :], axis=AX.X, op=ALU.add),
                wait=[("act", a_lat)] + WF("latstat"), inc="dve")
            EMIT("vector", lambda e: e.tensor_tensor(
                sqb[:, 0:L], latb[:, 0, :], latb[:, 0, :], op=ALU.mult), inc="dve")
            CH("vector", lambda e: e.tensor_reduce(
                statb[:, 13:14], sqb[:, 0:L], axis=AX.X, op=ALU.add))
            CH("vector", lambda e: e.tensor_scalar_mul(
                statb[:, 12:13], statb[:, 12:13], 1.0 / L))
            CH("vector", lambda e: e.tensor_tensor(
                statb[:, 14:15], statb[:, 12:13], statb[:, 12:13], op=ALU.mult))
            v_var = CH("vector", lambda e: e.scalar_tensor_tensor(
                statb[:, 14:15], statb[:, 13:14], 1.0 / L, statb[:, 14:15],
                op0=ALU.mult, op1=ALU.subtract))
            a_sig = CH("scalar", lambda e: e.activation(
                statb[:, 14:15], statb[:, 14:15], AF.Sqrt, bias=epsb[:]),
                wait=[("dve", v_var)])
            CH("vector", lambda e: e.reciprocal(statb[:, 15:16], statb[:, 14:15]),
                wait=[("act", a_sig)])
            CH("vector", lambda e: e.tensor_tensor(
                statb[:, 12:13], statb[:, 12:13], statb[:, 15:16], op=ALU.mult))
            v_nmr = CH("vector", lambda e: e.tensor_scalar_mul(
                statb[:, 12:13], statb[:, 12:13], -1.0))
            a_z = CH("scalar", lambda e: e.activation(
                latb[:, 1, :], latb[:, 0, :], AF.Identity, bias=statb[:, 12:13],
                scale=statb[:, 15:16]), wait=[("dve", v_nmr)])
            SF("latstat", ("act", a_z))
            CH("vector", lambda e: e.tensor_tensor(
                latb[:, 1, :], latb[:, 1, :], gbb[:, 0, :], op=ALU.mult),
                wait=[("act", a_z)] + WF("latn_t"))
            v_out = CH("vector", lambda e: e.tensor_tensor(
                latn_t[:], latb[:, 1, :], gbb[:, 1, :], op=ALU.add))
            return v_out

        def resid_tile(g, base_load_d, tscr_d, logits, xs_store_d, have_delta,
                       delta_src=None, psum_pair=None):
            s = g % 2
            pp = g % 2
            ld = DMA("sync", xin[:, s, :], base_load_d[g], wait=WF(("xin", s)),
                     inc="dx")
            if have_delta == "buf":
                tt = g % CT
                EMIT("vector", lambda e, s=s, tt=tt, pp=pp: e.tensor_tensor(
                    x1b[:, pp, :], xin[:, s, :], delta_src[:, tt, :], op=ALU.add),
                    wait=[("dx", ld)] + WF(("x1b", pp)), inc="dve")
            elif have_delta == "psum":
                pA, pB, tok = psum_pair
                EMIT("vector", lambda e, s=s, pp=pp, pA=pA: e.tensor_tensor(
                    x1b[:, pp, 0:512], xin[:, s, 0:512], pA[:], op=ALU.add),
                    wait=[("dx", ld)] + WF(("x1b", pp)) + list(tok), inc="dve")
                EMIT("vector", lambda e, s=s, pp=pp, pB=pB: e.tensor_tensor(
                    x1b[:, pp, 512:1024], xin[:, s, 512:1024], pB[:], op=ALU.add),
                    inc="dve")
            v_x = V("dve")
            SF(("xin", s), ("dve", v_x))
            st_x = DMA("sync", xs_store_d[g], x1b[:, pp, :], wait=[("dve", v_x)],
                       inc="dst")
            a_t = ln_normalize(x1b[:, pp, :], t2b[:, pp, :], pp, wait0=WF(("t2b", pp)))
            SF(("x1b", pp), ("act", a_t), ("dst", st_x))
            sg = g % 2
            v_tp, p_tp = transpose8(
                t2b[:, pp, :], lambda k, sg=sg: tstage[:, sg, k, :],
                wait_pe=[("act", a_t)], wait_dve=WF(("tstage", sg)))
            SF(("t2b", pp), ("pe", p_tp))
            st_t = DMA("sync", tscr_d[:, :, 128 * g:128 * (g + 1)], tstage[:, sg, :, :],
                       wait=[("dve", v_tp)], inc="dst")
            if logits is not None:
                pv = gating_logits(lambda k, sg=sg: tstage[:, sg, k, :],
                                   logits[:, g, :], pe_wait=[("dve", v_tp)])
                SF(("tstage", sg), ("dst", st_t), ("pe", V("pe")))
            else:
                SF(("tstage", sg), ("dst", st_t))
            return dict(v_x=v_x, st_x=st_x, st_t=st_t, v_tp=v_tp, sg=sg)

        w1_idx = [0]

        def moe_call(sc, ceb, ce_ready_c, epi_items_fn, free_items_fn,
                     tsrc_d, t_stored_c, post_chunk_fn=lambda c: None):
            w1r, w2r, aT, sgbL, delta, t1c = (sc["w1r"], sc["w2r"], sc["aT"],
                                              sc["sgb"], sc["delta"], sc["t1c"])
            t1c_ld = {}

            def ld_chunk(c):
                if c >= NCH:
                    return
                t1c_ld[c] = DMA("scalar", t1c[:, c % 2, :, :],
                                tsrc_d[:, :, CHUNK * c:CHUNK * (c + 1)],
                                wait=WF(("t1c", c % 2)) + [("dst", t_stored_c[c])],
                                inc="dxs")

            groups = [(c, e_i, g) for c in range(NCH) for e_i in range(E)
                      for g in range(4)]
            w1_ld = {}

            def emit_group_dma(gi):
                if gi >= len(groups):
                    return
                c, e_i, g = groups[gi]
                slot = gi % 2
                ld = DMA("gpsimd", w1r[:, slot, :, :, :], w1_d[e_i, g],
                         wait=WF(("w1", slot)), inc="dw")
                w1_ld[gi] = (slot, ld)

            ld_chunk(0)
            ld_chunk(1)
            for gi0 in range(2):
                emit_group_dma(gi0)
            gidx = [0]
            for c in range(NCH):
                tc = t1c[:, c % 2, :, :]
                epi_q = list(epi_items_fn(c))
                free_q = list(free_items_fn(c))
                for e_i in range(E):
                    w2_slot = e_i % 2
                    w2_ld = DMA("gpsimd", w2r[:, w2_slot, :, :], w2_d[e_i],
                                wait=WF(("w2", w2_slot)), inc="dw")
                    for mh in range(8):
                        if mh % 2 == 0:
                            gslot, gld = w1_ld.pop(gidx[0])
                        r = mh % 2
                        tg, tu = 2 * r, 2 * r + 1
                        ld_g = ld_u = gld
                        for nb in range(2):
                            gpar = (mh * 2 + nb) % 2
                            pgp = Pp[gpar]
                            for k in range(KD):
                                wl = []
                                if k == 0:
                                    wl = ([("dw", ld_g), ("dxs", t1c_ld[c])]
                                          + WF(("psum", gpar)))
                                EMIT("tensor", lambda e, k=k, gslot=gslot, tg=tg,
                                     nb=nb, pgp=pgp, tc=tc: e.matmul(
                                         pgp[:], w1r[:, gslot, tg, k, :],
                                         tc[:, k, 512 * nb:512 * (nb + 1)],
                                         start=(k == 0), stop=(k == KD - 1)),
                                     wait=wl, inc=("pe" if k == KD - 1 else None))
                            p_g = V("pe")
                            for k in range(KD):
                                wl = []
                                if k == 0:
                                    wl = [("dw", ld_u)] + WF(("psum", 2))
                                EMIT("tensor", lambda e, k=k, gslot=gslot, tu=tu,
                                     nb=nb, tc=tc: e.matmul(
                                         Pp[2][:], w1r[:, gslot, tu, k, :],
                                         tc[:, k, 512 * nb:512 * (nb + 1)],
                                         start=(k == 0), stop=(k == KD - 1)),
                                     wait=wl, inc=("pe" if k == KD - 1 else None))
                            p_u = V("pe")
                            if nb == 1 and mh % 2 == 1:
                                SF(("w1", gslot), ("pe", p_u))
                            EMIT("scalar", lambda e, pgp=pgp, gpar=gpar: e.activation(
                                sgbL[:, gpar, :], pgp[:], AF.Silu),
                                wait=[("pe", p_g)] + WF(("sgb", gpar)), inc="act")
                            a_sg = V("act")
                            SF(("psum", gpar), ("act", a_sg))
                            EMIT("vector", lambda e, mh=mh, nb=nb, gpar=gpar, aT=aT:
                                 e.tensor_tensor(
                                     aT[:, mh, 512 * nb:512 * (nb + 1)],
                                     Pp[2][:], sgbL[:, gpar, :], op=ALU.mult),
                                 wait=[("pe", p_u), ("act", a_sg)], inc="dve")
                            v_am = V("dve")
                            SF(("psum", 2), ("dve", v_am))
                            SF(("sgb", gpar), ("dve", v_am))
                        if e_i == 0 and epi_q:
                            epi_q.pop(0)()
                        elif free_q and (mh % 3 == 2):
                            free_q.pop(0)()
                        if mh % 2 == 1:
                            emit_group_dma(gidx[0] + 2)
                            gidx[0] += 1
                    if e_i == 0:
                        while epi_q:
                            epi_q.pop(0)()
                    v_a = V("dve")
                    for tt in range(CT):
                        for nb in range(2):
                            pi2 = 3 + 2 * (tt % 2) + nb
                            pop = Pp[pi2]
                            for k in range(KD):
                                wl = []
                                if k == 0:
                                    wl = ([("dve", v_a), ("dw", w2_ld)]
                                          + WF(("psum", pi2)))
                                EMIT("tensor", lambda e, k=k, tt=tt, nb=nb, pop=pop,
                                     w2_slot=w2_slot, w2r=w2r, aT=aT: e.matmul(
                                         pop[:], aT[:, k, 128 * tt:128 * (tt + 1)],
                                         w2r[:, w2_slot, k, 512 * nb:512 * (nb + 1)],
                                         start=(k == 0), stop=(k == KD - 1)),
                                     wait=wl, inc=("pe" if k == KD - 1 else None))
                            p_o = V("pe")
                            if tt == CT - 1 and nb == 1:
                                SF(("w2", w2_slot), ("pe", p_o))
                            gtile = c * CT + tt
                            if e_i == 0:
                                EMIT("vector", lambda e, tt=tt, nb=nb, pop=pop,
                                     gtile=gtile, ceb=ceb, delta=delta: e.tensor_scalar(
                                         delta[:, tt, 512 * nb:512 * (nb + 1)], pop[:],
                                         ceb[:, gtile, 0:1], None, op0=ALU.mult),
                                     wait=[("pe", p_o),
                                           ("dve", ce_ready_c[c])],
                                     inc="dve")
                            else:
                                EMIT("vector", lambda e, tt=tt, nb=nb, pop=pop,
                                     gtile=gtile, e_i=e_i, ceb=ceb, delta=delta:
                                     e.scalar_tensor_tensor(
                                         delta[:, tt, 512 * nb:512 * (nb + 1)], pop[:],
                                         ceb[:, gtile, e_i:e_i + 1],
                                         delta[:, tt, 512 * nb:512 * (nb + 1)],
                                         op0=ALU.mult, op1=ALU.add),
                                     wait=[("pe", p_o)], inc="dve")
                            SF(("psum", pi2), ("dve", V("dve")))
                while free_q:
                    free_q.pop(0)()
                SF(("t1c", c % 2), ("pe", V("pe")))
                post_chunk_fn(c)
                ld_chunk(c + 2)
            for it in epi_items_fn(NCH):
                it()

        def rope_tile(g, a_ready, sintok, costok, qp):
            cosb = costok[:, g:g + 1, :].to_broadcast([128, 16, 32])
            sinb = sintok[:, g:g + 1, :].to_broadcast([128, 16, 32])
            x1a = qt[:, qp, :].rearrange("p (h two d) -> p h two d", h=16, two=2)[:, :, 0, :]
            x2a = qt[:, qp, :].rearrange("p (h two d) -> p h two d", h=16, two=2)[:, :, 1, :]
            o1 = qkb[:, qp, :].rearrange("p (h two d) -> p h two d", h=16, two=2)[:, :, 0, :]
            o2 = qkb[:, qp, :].rearrange("p (h two d) -> p h two d", h=16, two=2)[:, :, 1, :]
            r1 = rtmp[:, 0, :].rearrange("p (h d) -> p h d", h=16)
            r2 = rtmp[:, 1, :].rearrange("p (h d) -> p h d", h=16)
            r3 = rtmp[:, 2, :].rearrange("p (h d) -> p h d", h=16)
            r4 = rtmp[:, 3, :].rearrange("p (h d) -> p h d", h=16)
            EMIT("vector", lambda e: e.tensor_tensor(r1, x1a, cosb, op=ALU.mult),
                 wait=[("act", a_ready)] + WF(("qkb", qp)), inc="dve")
            EMIT("vector", lambda e: e.tensor_tensor(r2, x2a, sinb, op=ALU.mult),
                 inc="dve")
            EMIT("vector", lambda e: e.tensor_tensor(r3, x1a, sinb, op=ALU.mult),
                 inc="dve")
            v4 = EMIT("vector", lambda e: e.tensor_tensor(r4, x2a, cosb, op=ALU.mult),
                      inc="dve")
            EMIT("vector", lambda e: e.tensor_tensor(o1, r1, r2, op=ALU.subtract),
                 wait=[("dve", v4 - 2)], inc="dve")
            EMIT("vector", lambda e: e.tensor_tensor(o2, r3, r4, op=ALU.add),
                 wait=[("dve", v4)], inc="dve")
            return V("dve")

        with ExitStack() as scA:
            sba = SBF(scA)
            t1c = sba("t1c", [128, 2, KD, CHUNK], BF16)
            delta = sba("delta", [128, CT, D], F32)
            aT = sba("aT", [128, KD, CHUNK], BF16)
            w1r = sba("w1r", [128, 2, 4, KD, 128], BF16)
            w2r = sba("w2r", [128, 2, KD, FH], BF16)
            sgbA = sba("sgbA", [128, 2, 512], F32)
            scbufs = dict(w1r=w1r, w2r=w2r, aT=aT, sgb=sgbA, delta=delta, t1c=t1c)

            t1_stored_c = {}
            ce1_ready_c = {}

            def p0_tile(g):
                sx = g % 2
                pp = g % 2
                ld = DMA("sync", xin[:, sx, :], x0_d[g], wait=WF(("xin", sx)),
                         inc="dx")
                a_t1 = ln_normalize(xin[:, sx, :], t2b[:, pp, :], pp,
                                    wait0=[("dx", ld)] + WF(("t2b", pp)))
                SF(("xin", sx), ("act", a_t1), ("dve", V("dve")))
                sg = g % 2
                v_tp, p_tp = transpose8(
                    t2b[:, pp, :], lambda k, sg=sg: tstage[:, sg, k, :],
                    wait_pe=[("act", a_t1)], wait_dve=WF(("tstage", sg)))
                SF(("t2b", pp), ("pe", p_tp))
                st_t = DMA("sync", t1s_d[:, :, 128 * g:128 * (g + 1)],
                           tstage[:, sg, :, :], wait=[("dve", v_tp)], inc="dst")
                gating_logits(lambda k, sg=sg: tstage[:, sg, k, :],
                              logit1[:, g, :], pe_wait=[("dve", v_tp)])
                SF(("tstage", sg), ("dst", st_t), ("pe", V("pe")))

            def p0_record(c0):
                t1_stored_c[c0] = V("dst")
                for tt in range(CT):
                    gating_epilogue(logit1, ceb1, c0 * CT + tt)
                ce1_ready_c[c0] = V("dve")

            for c0 in range(2):
                for tt in range(CT):
                    p0_tile(c0 * CT + tt)
                p0_record(c0)

            def p0_free_items(c):
                c2 = c + 2
                if c2 >= NCH:
                    return []
                items = [(lambda g=g: p0_tile(g)) for g in
                         range(c2 * CT, (c2 + 1) * CT)]
                items.append(lambda c2=c2: p0_record(c2))
                return items

            def p1_tile(c, tt):
                g = c * CT + tt
                rt = resid_tile(g, x0_d, t2s_d, None, xs1_d, "buf",
                                delta_src=delta)
                sg = rt["sg"]
                for k in range(KD):
                    wl = (WF(("psum", 2)) + [("dve", rt["v_tp"])]) if k == 0 else []
                    EMIT("tensor", lambda e, k=k, sg=sg: e.matmul(
                        Pp[2][:, 0:L], tstage[:, sg, k, :], wdkvs[:, k, :],
                        start=(k == 0), stop=(k == KD - 1)),
                        wait=wl, inc=("pe" if k == KD - 1 else None))
                p_lat = V("pe")
                SF(("tstage", sg), *(WF(("tstage", sg)) + [("pe", p_lat)]))
                EMIT("scalar", lambda e: e.copy(latb[:, 0, :], Pp[2][:, 0:L]),
                     wait=[("pe", p_lat)] + WF("latb"), inc="act")
                a_lat = V("act")
                SF(("psum", 2), ("act", a_lat))
                v_latn = lat_ln(a_lat)
                SF("latb", ("dve", v_latn))
                for kk in range(KL):
                    EMIT("tensor", lambda e, kk=kk: e.transpose(
                        PT[:, 0:128], latn_t[:, 128 * kk:128 * (kk + 1)], eyeb[:]),
                        wait=([("dve", v_latn)] if kk == 0 else []) + WF("PT"),
                        inc="pe")
                    p_tp = V("pe")
                    EMIT("vector", lambda e, kk=kk, g=g: e.tensor_copy(
                        latnT[:, kk, 128 * g:128 * (g + 1)], PT[:, 0:128]),
                        wait=[("pe", p_tp)], inc="dve")
                    SF("PT", ("dve", V("dve")))
                SF("latn_t", ("pe", V("pe")))

            def p1_epi_items(c):
                if c == 0:
                    return []
                cm = c - 1
                return [(lambda tt=tt, cm=cm: p1_tile(cm, tt)) for tt in range(CT)]

            moe_call(scbufs, ceb1, ce1_ready_c, p1_epi_items, p0_free_items,
                     t1s_d, t1_stored_c)
            pb.barrier()
        a_end = {s: V(s) for s in ("pe", "dve", "act", "dst", "dx", "dw")}

        def dump_dram_f32(src_d):
            for g in range(NT):
                s = g % 2
                ld = DMA("sync", xin[:, s, :], src_d[g], wait=WF(("xin", s)), inc="dx")
                EMIT("vector", lambda e, s=s, g=g: e.tensor_copy(
                    x1b[:, g % 2, :], xin[:, s, :]),
                    wait=[("dx", ld)] + WF(("x1b", g % 2)), inc="dve")
                v_y = V("dve")
                SF(("xin", s), ("dve", v_y))
                st = DMA("sync", y_d[g], x1b[:, g % 2, :], wait=[("dve", v_y)],
                         inc="dst")
                SF(("x1b", g % 2), ("dst", st))
            EMIT("sync", fn=None, wait=[("dst", V("dst"))])

        def dump_dram_bf16(src_d):
            for g in range(NT):
                sg = g % 2
                ld = DMA("sync", tstage[:, sg, :, :], src_d[:, :, 128 * g:128 * (g + 1)],
                         wait=WF(("tstage", sg)), inc="dx")
                EMIT("vector", lambda e, sg=sg: e.tensor_copy(
                    x1b[:, sg, :], tstage[:, sg, :, :].rearrange("p k t -> p (k t)")),
                    wait=[("dx", ld)] + WF(("x1b", sg)), inc="dve")
                v_y = V("dve")
                SF(("tstage", sg), ("dve", v_y))
                st = DMA("sync", y_d[g], x1b[:, sg, :], wait=[("dve", v_y)], inc="dst")
                SF(("x1b", sg), ("dst", st))
            EMIT("sync", fn=None, wait=[("dst", V("dst"))])

        if upto not in ("x1", "t1", "t2", "ce"):
            pass
        if upto not in ("x1", "t1", "t2", "ce"):
            with ExitStack() as scB:
                sbb = SBF(scB)
                rtmp = sbb("rtmp", [128, 4, 512], F32)
                qt = sbb("qt", [128, 2, D], F32)
                qkb = sbb("qkb", [128, 2, D], BF16)
                t2c = sbb("t2c", [128, KD, 512], BF16)
                wqs = sbb("wqs", [128, KD, D], BF16)
                wuvs = sbb("wuvs", [128, KL, D], BF16)
                wos = sbb("wos", [128, KD, D], BF16)
                sintok = sbb("sintok", [128, NT, 32], F32)
                costok = sbb("costok", [128, NT, 32], F32)
                maskb = sbb("maskb", [128, 4, 512], BF16)
                qhb = sbb("qhb", [64, 2, T], BF16)
                khb = sbb("khb", [64, 2, T], BF16)
                vhb = sbb("vhb", [128, 2, NT, 65], BF16)
                eTb = sbb("eTb", [128, 6, 512], BF16)
                mparts = sbb("mparts", [128, 8], F32)
                mrun = sbb("mrun", [128, 1], F32)
                mhsc = sbb("mhsc", [1, 4], F32)
                qsq = sbb("qsq", [64, T // 2], BF16)
                onesc = sbb("onesc", [64, 1], BF16)
                negm = sbb("negm", [128, 2], F32)
                rcrow = sbb("rcrow", [1, 512], F32)
                rbs = sbb("rbs", [64, 512], F32)
                atmp = sbb("atmp", [64, 512], BF16)
                aoc = sbb("aoc", [128, KD, 512], BF16)

                DMA("gpsimd", wqs[:], wq_d, inc="dw")
                DMA("gpsimd", wuvs[:], wuv_d, inc="dw")
                DMA("gpsimd", wos[:], wo_d, inc="dw")
                ld_bw = V("dw")

                for d in range(4):
                    EMIT("vector", lambda e, d=d: e.tensor_single_scalar(
                        maskb[:, d, :], iot[:], float(128 * d), op=ALU.is_ge),
                        inc="dve")
                for t in range(NT):
                    g_pos = CH("gpsimd", lambda e, t=t: e.iota(
                        posr[:], pattern=[[1, 128]], base=128 * t,
                        channel_multiplier=0, allow_small_or_imprecise_dtypes=True))
                    EMIT("tensor", lambda e: e.matmul(
                        Pp[6][:, 0:32], posr[0:1, :], invfs[0:1, :],
                        start=True, stop=True),
                        wait=[("pool", g_pos)] + WF(("psum", 6)), inc="pe")
                    pv = V("pe")
                    EMIT("vector", lambda e: e.tensor_copy(angb[:, 0, :], Pp[6][:, 0:32]),
                         wait=[("pe", pv)] + WF("angb"), inc="dve")
                    SF(("psum", 6), ("dve", V("dve")))
                    for dst, shift in ((sintok, 0.0), (costok, HALFPI)):
                        if shift:
                            CH("vector", lambda e: e.tensor_scalar_add(
                                angb[:, 0, :], angb[:, 0, :], shift))
                        CH("vector", lambda e: e.tensor_scalar(
                            angb[:, 1, :], angb[:, 0, :], INV2PI, MAGIC,
                            op0=ALU.mult, op1=ALU.add))
                        CH("vector", lambda e: e.tensor_scalar_sub(
                            angb[:, 1, :], angb[:, 1, :], MAGIC))
                        CH("vector", lambda e, dst=dst, t=t: e.scalar_tensor_tensor(
                            dst[:, t, :], angb[:, 1, :], -RC1, angb[:, 0, :],
                            op0=ALU.mult, op1=ALU.add))
                        v_ang = CH("vector", lambda e, dst=dst, t=t: e.scalar_tensor_tensor(
                            dst[:, t, :], angb[:, 1, :], -RC2, dst[:, t, :],
                            op0=ALU.mult, op1=ALU.add))
                    a_s1 = CH("scalar", lambda e, t=t: e.activation(
                        sintok[:, t, :], sintok[:, t, :], AF.Sin),
                        wait=[("dve", v_ang)])
                    a_s2 = CH("scalar", lambda e, t=t: e.activation(
                        costok[:, t, :], costok[:, t, :], AF.Sin))
                    SF("angb", ("act", a_s2))
                tabs_done_a = V("act")

                qk_par = [0]

                def qk_proj(g, lhsT_fn, nk, rhs_w, scr_d, extra_wait):
                    qp = qk_par[0] % 2
                    qk_par[0] += 1
                    for nb in range(2):
                        pq = Pp[nb]
                        for k in range(nk):
                            wl = []
                            if k == 0:
                                wl = (list(extra_wait) + WF(("psum", nb))
                                      + [("dw", ld_bw)])
                            EMIT("tensor", lambda e, k=k, nb=nb, pq=pq: e.matmul(
                                pq[:], lhsT_fn(k),
                                rhs_w[:, k, 512 * nb:512 * (nb + 1)],
                                start=(k == 0), stop=(k == nk - 1)),
                                wait=wl, inc=("pe" if k == nk - 1 else None))
                        p_q = V("pe")
                        EMIT("scalar", lambda e, nb=nb, pq=pq, qp=qp: e.copy(
                            qt[:, qp, 512 * nb:512 * (nb + 1)], pq[:]),
                            wait=[("pe", p_q)] + WF(("qt", qp)), inc="act")
                        SF(("psum", nb), ("act", V("act")))
                    a_qt = V("act")
                    v_rope = rope_tile(g, a_qt, sintok, costok, qp)
                    SF(("qt", qp), ("dve", v_rope))
                    sg = qp
                    v_tp, p_tp = transpose8(
                        qkb[:, qp, :],
                        lambda k, sg=sg: tstage[:, sg, k, :],
                        wait_pe=[("dve", v_rope), ("act", tabs_done_a)],
                        wait_dve=WF(("tstage", sg)))
                    SF(("qkb", qp), ("pe", p_tp))
                    st = DMA("sync", scr_d[:, :, 128 * g:128 * (g + 1)],
                             tstage[:, sg, :, :], wait=[("dve", v_tp)], inc="dst")
                    SF(("tstage", sg), ("dst", st))
                    return st

                qk_stored = 0
                for half in range(T // 512):
                    ld_t2 = DMA("scalar", t2c[:, :, :],
                                t2s_d[:, :, 512 * half:512 * (half + 1)],
                                wait=WF("t2c"), inc="dxs")
                    for tt in range(4):
                        g = 4 * half + tt
                        qk_proj(g, lambda k, tt=tt: t2c[:, k, 128 * tt:128 * (tt + 1)],
                                KD, wqs, qscr_d, [("dxs", ld_t2)])
                        qk_stored = qk_proj(
                            g, lambda k, g=g: latnT[:, k, 128 * g:128 * (g + 1)],
                            KL, wuks, kscr_d, [])
                    SF("t2c", ("pe", V("pe")))

                PMh = PB2
                PAVs = [PS6, PB2[:, 512:1024]]
                SF(("pA",), *(WF(("psum", 0)) + WF(("psum", 1))))
                SF(("pB",), *(WF(("psum", 2)) + WF(("psum", 3))))
                SF(("pM", 0), *WF(("psum", 4)))
                SF(("pAV", 1), *WF(("psum", 5)))
                SF(("pAV", 0), *WF(("psum", 6)))
                EMIT("vector", lambda e: e.memset(vhb[:, 0, :, 64:65], 1.0), inc="dve")
                EMIT("vector", lambda e: e.memset(vhb[:, 1, :, 64:65], 1.0), inc="dve")
                EMIT("vector", lambda e: e.memset(onesc[:], 1.0), inc="dve")

                ld_q, ld_k = {}, {}

                def head_loads(hn):
                    if hn >= H:
                        return
                    hp = hn % 2
                    kd_h, off = hn // 2, 64 * (hn % 2)
                    ld_q[hn] = DMA("sync", qhb[:, hp, :],
                                   qscr_d[off:off + 64, kd_h, :],
                                   wait=WF(("qh", hp)) +
                                   ([("dst", qk_stored)] if hn < 2 else []), inc="dx")
                    ld_k[hn] = DMA("sync", khb[:, hp, :],
                                   kscr_d[off:off + 64, kd_h, :],
                                   wait=WF(("kh", hp)), inc="dx")

                pm_rr = [0]
                sq_tok = {}

                def emit_item(hn, item):
                    hp = hn % 2
                    kind = item[0]
                    par = 0
                    pmap = PMh[:, 0:512]
                    if kind == "V":
                        t0i = item[1]
                        for i4 in range(4):
                            tt = t0i + i4
                            for kl in range(KL):
                                wl = (WF(("pM", par)) if (kl == 0 and i4 == 0) else [])
                                EMIT("tensor", lambda e, kl=kl, tt=tt, hn=hn, i4=i4,
                                     pmap=pmap: e.matmul(
                                         pmap[:, 64 * i4:64 * (i4 + 1)],
                                         latnT[:, kl, 128 * tt:128 * (tt + 1)],
                                         wuvs[:, kl, 64 * hn:64 * (hn + 1)],
                                         start=(kl == 0), stop=(kl == KL - 1)),
                                     wait=wl,
                                     inc=("pe" if (kl == KL - 1 and i4 == 3) else None))
                        p_v = V("pe")
                        EMIT("scalar", lambda e, t0i=t0i, hp=hp, pmap=pmap: e.copy(
                            vhb[:, hp, t0i:t0i + 4, 0:64],
                            pmap[:, 0:256].rearrange("p (a d) -> p a d", a=4)),
                            wait=[("pe", p_v)] + WF(("vh", hp, t0i)), inc="act")
                        SF(("pM", par), ("act", V("act")))
                    elif kind == "SQ":
                        which, hf = item[1], item[2]
                        src = qhb if which == "q" else khb
                        wl = ([("dx", (ld_q if which == "q" else ld_k)[hn])]
                              + WF("qsq"))
                        sq_tok[which] = EMIT(
                            "vector", lambda e, src=src, hp=hp, hf=hf:
                            e.tensor_tensor(
                                qsq[:], src[:, hp, 2048 * hf:2048 * (hf + 1)],
                                src[:, hp, 2048 * hf:2048 * (hf + 1)], op=ALU.mult),
                            wait=wl, inc="dve")
                    elif kind == "NRM":
                        which, j = item[1], item[2]
                        v_sq = sq_tok[which]
                        EMIT("tensor", lambda e, j=j, pmap=pmap: e.matmul(
                            pmap[0:1, 0:512], onesc[:],
                            qsq[:, 512 * (j % 4):512 * (j % 4 + 1)],
                            start=True, stop=True),
                            wait=[("dve", v_sq)] + WF(("pM", par)), inc="pe")
                        p_n = V("pe")
                        col = 0 if which == "q" else 1
                        EMIT("vector", lambda e, pmap=pmap: e.tensor_reduce(
                            mhsc[0:1, 2:3], pmap[0:1, 0:512], axis=AX.X, op=ALU.max),
                            wait=[("pe", p_n)], inc="dve")
                        SF(("pM", par), ("dve", V("dve")))
                        if j == 0:
                            CH("vector", lambda e, col=col: e.tensor_copy(
                                mhsc[0:1, col:col + 1], mhsc[0:1, 2:3]))
                        else:
                            CH("vector", lambda e, col=col: e.tensor_tensor(
                                mhsc[0:1, col:col + 1], mhsc[0:1, col:col + 1],
                                mhsc[0:1, 2:3], op=ALU.max))
                        if j % 4 == 3:
                            SF("qsq", ("pe", p_n))
                    elif kind == "MNB":
                        v_p = CH("vector", lambda e: e.tensor_tensor(
                            mhsc[0:1, 2:3], mhsc[0:1, 0:1], mhsc[0:1, 1:2],
                            op=ALU.mult))
                        a_sq = CH("scalar", lambda e: e.activation(
                            mhsc[0:1, 2:3], mhsc[0:1, 2:3], AF.Sqrt),
                            wait=[("dve", v_p)])
                        v_mh = CH("vector", lambda e: e.tensor_scalar_mul(
                            mhsc[0:1, 3:4], mhsc[0:1, 2:3], -0.125),
                            wait=[("act", a_sq)])
                        EMIT("tensor", lambda e, pmap=pmap: e.matmul(
                            pmap[:, 0:1], onesr[0:1, :], mhsc[0:1, 3:4],
                            start=True, stop=True),
                            wait=[("dve", v_mh)] + WF(("pM", 0)), inc="pe")
                        p_nb = V("pe")
                        EMIT("vector", lambda e, hp=hp, pmap=pmap: e.tensor_copy(
                            negm[:, hp:hp + 1], pmap[:, 0:1]),
                            wait=[("pe", p_nb)], inc="dve")
                        SF(("pM", par), ("dve", V("dve")))
                        SF(("negm", hp), ("dve", V("dve")))

                def make_items(hn):
                    items = []
                    if hn >= H:
                        return items
                    for tt in range(0, NT, 4):
                        items.append(("V", tt))
                    for which in ("q", "k"):
                        for hf in range(2):
                            items.append(("SQ", which, hf))
                            for j in range(4 * hf, 4 * hf + 4):
                                items.append(("NRM", which, j))
                    items.append(("MNB",))
                    return items

                head_loads(0)
                head_loads(1)
                for it in make_items(0):
                    emit_item(0, it)
                for h in range(H):
                    hp = h % 2
                    kd_h, off = h // 2, 64 * (h % 2)
                    items = make_items(h + 1)
                    qi = [0]

                    def drip(n=3):
                        for _ in range(n):
                            if qi[0] < len(items):
                                emit_item(h + 1, items[qi[0]])
                                qi[0] += 1

                    v_negm = [t for t in WF(("negm", hp))]
                    pend_epi = [None, None]

                    def emit_epi_a():
                        if pend_epi[0] is None:
                            return
                        PAVe, avkeye, st_e = pend_epi[0]
                        pend_epi[0] = None
                        EMIT("vector", lambda e, PAVe=PAVe: e.tensor_scalar_max(
                            rcrow[:], PAVe[64:65, :], 1.0e-30),
                            wait=WF(("epi_pav", st_e % 2)) + WF("rcrow"), inc="dve")
                        v_rc = CH("vector", lambda e: e.reciprocal(
                            rcrow[:], rcrow[:]))
                        pend_epi[1] = (PAVe, avkeye, st_e, v_rc)

                    def emit_epi_b():
                        if pend_epi[1] is None:
                            return
                        PAVe, avkeye, st_e, v_rc = pend_epi[1]
                        pend_epi[1] = None
                        pmap3 = PMh[:, 0:512]
                        EMIT("tensor", lambda e, pmap3=pmap3: e.matmul(
                            pmap3[0:64, :], onesr[0:1, 0:64], rcrow[0:1, :],
                            start=True, stop=True),
                            wait=[("dve", v_rc)] + WF(("pM", 0)), inc="pe")
                        p_rb = V("pe")
                        SF("rcrow", ("pe", p_rb))
                        EMIT("vector", lambda e, pmap3=pmap3: e.tensor_copy(
                            rbs[:], pmap3[0:64, :]),
                            wait=[("pe", p_rb)] + WF("rbs"), inc="dve")
                        SF(("pM", 0), ("dve", V("dve")))
                        v_at = CH("vector", lambda e, PAVe=PAVe: e.tensor_tensor(
                            atmp[:], PAVe[0:64, :], rbs[:], op=ALU.mult),
                            wait=WF("atmp"))
                        SF(avkeye, ("dve", v_at))
                        SF("rbs", ("dve", v_at))
                        st_ao = DMA("sync",
                                    aos_d[off:off + 64, kd_h,
                                          512 * st_e:512 * (st_e + 1)],
                                    atmp[:], wait=[("dve", v_at)], inc="dst")
                        SF("atmp", ("dst", st_ao))

                    def emit_epi():
                        emit_epi_a()
                        emit_epi_b()

                    for st_i in range(8):
                        jmax = 4 * (st_i + 1)
                        npairs = jmax // 2
                        PAV = PAVs[st_i % 2]
                        avkey = ("pAV", st_i % 2)
                        pend = None
                        for jp in range(npairs + 1):
                            if jp < npairs:
                                j0 = 2 * jp
                                ppar = jp % 2
                                PAx = PB0 if ppar == 0 else PB1
                                pkey = ("pA",) if ppar == 0 else ("pB",)
                                epar = jp % 3
                                for jj in (j0, j0 + 1):
                                    wl = []
                                    if jj == j0:
                                        wl = WF(pkey)
                                    EMIT("tensor", lambda e, jj=jj, st_i=st_i, hp=hp,
                                         PAx=PAx: e.matmul(
                                             PAx[:, 512 * (jj % 2):512 * (jj % 2 + 1)],
                                             khb[:, hp, 128 * jj:128 * (jj + 1)],
                                             qhb[:, hp, 512 * st_i:512 * (st_i + 1)],
                                             start=True, stop=True),
                                         wait=wl, inc=("pe" if jj > j0 else None))
                                p_s = V("pe")
                                ew = ([("pe", p_s)] + WF(("eTp", epar))
                                      + list(v_negm))
                                EMIT("scalar", lambda e, epar=epar, hp=hp, PAx=PAx:
                                     e.activation(
                                         eTb[:, 2 * epar:2 * epar + 2, :].rearrange(
                                             "p a t -> p (a t)"),
                                         PAx[:], AF.Exp,
                                         bias=negm[:, hp:hp + 1], scale=0.125),
                                     wait=ew, inc="act")
                                a_e = V("act")
                                SF(pkey, ("act", a_e))
                                toks = {}
                                for jj in (j0, j0 + 1):
                                    di = jj - 4 * st_i
                                    esl = 2 * epar + (jj - j0)
                                    if di >= 0:
                                        EMIT("vector", lambda e, esl=esl, di=di:
                                             e.tensor_tensor(
                                                 eTb[:, esl, :], eTb[:, esl, :],
                                                 maskb[:, di, :], op=ALU.mult),
                                             wait=[("act", a_e)], inc="dve")
                                        toks[jj] = ("dve", V("dve"))
                                    else:
                                        toks[jj] = ("act", a_e)
                                drip(1)
                                if jp == 1:
                                    emit_epi_a()
                                elif jp == 3:
                                    emit_epi_b()
                            if pend is not None:
                                pj0, pepar, ptoks = pend
                                for jj in (pj0, pj0 + 1):
                                    esl = 2 * pepar + (jj - pj0)
                                    wl = [ptoks[jj]]
                                    if jj == 0:
                                        wl += list(avkey and WF(avkey))
                                    EMIT("tensor", lambda e, jj=jj, hp=hp, esl=esl,
                                         jmax=jmax, PAV=PAV: e.matmul(
                                             PAV[0:65, :], vhb[:, hp, jj, 0:65],
                                             eTb[:, esl, :],
                                             start=(jj == 0), stop=(jj == jmax - 1)),
                                         wait=wl, inc="pe")
                                SF(("eTp", pepar), ("pe", V("pe")))
                                drip(1)
                            if jp < npairs:
                                pend = (j0, epar, toks)
                            elif jp == npairs:
                                emit_epi_b()
                        SF(("epi_pav", st_i % 2), ("pe", V("pe")))
                        pend_epi[0] = (PAV, avkey, st_i)
                    emit_epi()
                    while qi[0] < len(items):
                        emit_item(h + 1, items[qi[0]])
                        qi[0] += 1
                    p_heads_end = V("pe")
                    SF(("qh", hp), ("pe", p_heads_end))
                    SF(("kh", hp), ("pe", p_heads_end))
                    for tt in range(0, NT, 4):
                        SF(("vh", hp, tt), ("pe", p_heads_end))
                    head_loads(h + 2)
                SF(("psum", 0), *WF(("pA",)))
                SF(("psum", 1), *WF(("pA",)))
                SF(("psum", 2), *WF(("pB",)))
                SF(("psum", 3), *WF(("pB",)))
                SF(("psum", 4), *WF(("pM", 0)))
                SF(("psum", 5), *WF(("pAV", 1)))
                SF(("psum", 6), *WF(("pAV", 0)))
                aos_stored = V("dst")

                pend_res = [None]

                def flush_res():
                    if pend_res[0] is None:
                        return
                    g, pA, pB, p_o, pio0 = pend_res[0]
                    pend_res[0] = None
                    rt = resid_tile(g, xs1_d, t3s_d, logit3, xs2_d, "psum",
                                    psum_pair=(pA, pB, [("pe", p_o)]))
                    SF(("psum", pio0), ("dve", rt["v_x"]))
                    SF(("psum", pio0 + 1), ("dve", rt["v_x"]))

                for st_i in range(8):
                    ld_ao = DMA("scalar", aoc[:, :, :],
                                aos_d[:, :, 512 * st_i:512 * (st_i + 1)],
                                wait=WF("aoc") +
                                ([("dst", aos_stored)] if st_i == 0 else []),
                                inc="dxs")
                    for tl in range(4):
                        g = 4 * st_i + tl
                        p_o = None
                        for nb in range(2):
                            pio = 2 * (g % 2) + nb
                            pop = Pp[pio]
                            for k in range(KD):
                                wl = []
                                if k == 0:
                                    wl = [("dxs", ld_ao)] + WF(("psum", pio))
                                EMIT("tensor", lambda e, k=k, tl=tl, nb=nb, pop=pop:
                                     e.matmul(
                                         pop[:], aoc[:, k, 128 * tl:128 * (tl + 1)],
                                         wos[:, k, 512 * nb:512 * (nb + 1)],
                                         start=(k == 0), stop=(k == KD - 1)),
                                     wait=wl, inc=("pe" if k == KD - 1 else None))
                            p_o = V("pe")
                        pA, pB = Pp[2 * (g % 2)], Pp[2 * (g % 2) + 1]
                        nxt = (g, pA, pB, p_o, 2 * (g % 2))
                        flush_res()
                        pend_res[0] = nxt
                    SF("aoc", ("pe", V("pe")))
                flush_res()
                t3_stored = V("dst")
                for g in range(NT):
                    gating_epilogue(logit3, ceb3, g)
                v_ce3 = V("dve")
                t3_stored_c = {c: t3_stored for c in range(NCH)}
                ce3_ready_c = {c: v_ce3 for c in range(NCH)}
                pb.barrier()

            if upto == "x2":
                dump_dram_f32(xs2_d)
            elif upto == "ao":
                dump_dram_bf16(aos_d)
            else:
                with ExitStack() as scC:
                    sbc = SBF(scC)
                    t3c = sbc("t3c", [128, 2, KD, CHUNK], BF16)
                    deltaC = sbc("deltaC", [128, CT, D], F32)
                    aTC = sbc("aTC", [128, KD, CHUNK], BF16)
                    w1rC = sbc("w1rC", [128, 2, 4, KD, 128], BF16)
                    w2rC = sbc("w2rC", [128, 2, KD, FH], BF16)
                    sgbC = sbc("sgbC", [128, 2, 512], F32)
                    scbufsC = dict(w1r=w1rC, w2r=w2rC, aT=aTC, sgb=sgbC,
                                   delta=deltaC, t1c=t3c)

                    def p3_tile(c, tt):
                        g = c * CT + tt
                        sx = g % 2
                        ld = DMA("sync", xin[:, sx, :], xs2_d[g],
                                 wait=WF(("xin", sx)), inc="dx")
                        EMIT("vector", lambda e, sx=sx, tt=tt: e.tensor_tensor(
                            x1b[:, sx, :], xin[:, sx, :], deltaC[:, tt, :],
                            op=ALU.add),
                            wait=[("dx", ld)] + WF(("x1b", sx)), inc="dve")
                        v_y = V("dve")
                        SF(("xin", sx), ("dve", v_y))
                        st = DMA("sync", y_d[g], x1b[:, sx, :],
                                 wait=[("dve", v_y)], inc="dst")
                        SF(("x1b", sx), ("dst", st))

                    def p3_epi_items(c):
                        if c == 0:
                            return []
                        cm = c - 1
                        return [(lambda tt=tt, cm=cm: p3_tile(cm, tt))
                                for tt in range(CT)]

                    moe_call(scbufsC, ceb3, ce3_ready_c, p3_epi_items,
                             lambda c: [], t3s_d, t3_stored_c)
                    EMIT("sync", fn=None, wait=[("dst", V("dst"))])

        if upto == "x1":
            dump_dram_f32(xs1_d)
        elif upto == "t1":
            dump_dram_bf16(t1s_d)
        elif upto == "t2":
            dump_dram_bf16(t2s_d)
        elif upto == "ao":
            dump_dram_bf16(aos_d)
        elif upto == "ce":
            for g in range(NT):
                EMIT("vector", lambda e, g=g: e.tensor_copy(
                    x1b[:, g % 2, 0:E], ceb1[:, g, :]),
                    wait=WF(("x1b", g % 2)), inc="dve")
                EMIT("vector", lambda e, g=g: e.memset(x1b[:, g % 2, E:D], 0.0),
                     inc="dve")
                v_y = V("dve")
                st = DMA("sync", y_d[g], x1b[:, g % 2, :], wait=[("dve", v_y)],
                         inc="dst")
                SF(("x1b", g % 2), ("dst", st))
            EMIT("sync", fn=None, wait=[("dst", V("dst"))])

        with nc.Block() as block:
            for eng in ENGINES:
                def make(eng):
                    def body(e):
                        for f in pb.ops[eng]:
                            f(e)
                    return body
                getattr(block, eng)(make(eng))

    return nc


def _prep_weights(i):
    f32 = np.float32

    def pmaj(w, kt, x):
        return np.ascontiguousarray(
            np.asarray(w, dtype=f32).reshape(kt, 128, x).transpose(1, 0, 2))

    w1 = np.ascontiguousarray(i["L0_ffn_w1"], dtype=f32)
    w1b = np.ascontiguousarray(
        w1.reshape(E, 2, 4, 2, 128, KD, 128).transpose(0, 2, 6, 3, 1, 5, 4)
        .reshape(E, 4, 128, 4, KD, 128))
    w2 = np.ascontiguousarray(i["L0_ffn_w2"], dtype=f32)
    w2b = np.ascontiguousarray(
        w2.transpose(0, 2, 1).reshape(E, KD, 128, FH).transpose(0, 2, 1, 3))
    wgb = pmaj(np.asarray(i["L0_ffn_Wg"], dtype=f32).T, KD, E)
    bgb = np.ascontiguousarray(np.asarray(i["L0_ffn_bg"], dtype=f32).reshape(1, E))
    wqb = pmaj(i["L0_attn_W_q"], KD, D)
    wdkvb = pmaj(i["L0_attn_W_dkv"], KD, L)
    wukb = pmaj(i["L0_attn_W_uk"], KL, D)
    wuvb = pmaj(i["L0_attn_W_uv"], KL, D)
    wob = pmaj(i["L0_attn_W_o"], KD, D)
    gamb = np.ascontiguousarray(np.asarray(i["L0_attn_ln_gamma"], dtype=f32).reshape(1, L))
    betb = np.ascontiguousarray(np.asarray(i["L0_attn_ln_beta"], dtype=f32).reshape(1, L))
    invfb = (1.0 / (10000.0 ** (np.arange(32, dtype=np.float64) / 32))).astype(
        f32).reshape(1, 32)
    return dict(w1b=w1b, w2b=w2b, wgb=wgb, bgb=bgb, wqb=wqb, wdkvb=wdkvb,
                wukb=wukb, wuvb=wuvb, wob=wob, gamb=gamb, betb=betb, invfb=invfb)


def kernel(_upto="full", _cores=8, **inputs):
    x = np.ascontiguousarray(np.asarray(inputs["L0_attn_x"]), dtype=np.float32)
    wb = _prep_weights({k: np.asarray(v) for k, v in inputs.items()})
    nc = build_program(upto=_upto)
    in_maps = []
    for c in range(_cores):
        m = dict(wb)
        m["x0"] = np.ascontiguousarray(x[c].reshape(NT, 128, D))
        in_maps.append(m)
    res = run_bass_kernel_spmd(nc, in_maps, core_ids=list(range(_cores)))
    return np.stack([r["y"].reshape(T, D) for r in res.results])
